# version 04
# baseline (speedup 1.0000x reference)
import contextlib

import numpy as np
import ml_dtypes
import concourse.bass as bass
import concourse.mybir as mybir
from concourse.bass_utils import run_bass_kernel_spmd

F32 = mybir.dt.float32
BF16 = mybir.dt.bfloat16
AF = mybir.ActivationFunctionType
ALU = mybir.AluOpType
AX = mybir.AxisListType

NCORES = 8
S = 2
T = 2048
D = 1024
H = 16
INTER = 4096
NCH = D // 128
NIC = INTER // 128
FT = 256
NFT = T // FT
QT = 512
NQB = T // QT
NTC = T // 128
VW = H * 65

ENGS = ("sync", "tensor", "vector", "scalar", "gpsimd")


class Plan:

    def __init__(self, nc, sems):
        self.nc = nc
        self.sems = sems
        self.q = {e: [] for e in ENGS}
        self.cnt = {name: 0 for name in sems}
        self.lastwait = {e: {} for e in ENGS}

    def op(self, eng, meth, *args, waits=(), inc=None, inc_amt=1, **kw):
        self.wait_only(eng, waits)
        tok = None
        if inc is not None:
            self.cnt[inc] += inc_amt
            tok = (inc, self.cnt[inc])
        self.q[eng].append(("c", meth, args, kw, inc, inc_amt))
        return tok

    def dma(self, eng, out, in_, waits=()):
        sname = "sd_s" if eng == "sync" else "sd_g"
        return self.op(eng, "dma_start", out, in_, waits=waits, inc=sname, inc_amt=16)

    def wait_only(self, eng, waits):
        lw = self.lastwait[eng]
        for tok in waits:
            if tok is None:
                continue
            sname, val = tok
            if val <= 0 or lw.get(sname, -1) >= val:
                continue
            assert val <= self.cnt[sname], (eng, sname, val, self.cnt[sname])
            lw[sname] = val
            self.q[eng].append(("w", sname, val))

    def tok(self, sname):
        return (sname, self.cnt[sname])

    def barrier(self):
        toks = [(n, c) for n, c in self.cnt.items() if c > 0]
        for e in ENGS:
            self.wait_only(e, toks)

    def flush(self):
        qs = self.q
        self.q = {e: [] for e in ENGS}
        if not any(qs[e] for e in ENGS):
            return
        with self.nc.allow_low_precision(
                reason="bf16 operands / fp32 accumulation per problem tolerance"), \
                self.nc.Block() as blk:
            for e in ENGS:
                if not qs[e]:
                    continue

                def runner(engobj, items=qs[e]):
                    for it in items:
                        if it[0] == "w":
                            engobj.wait_ge(self.sems[it[1]], it[2])
                        else:
                            _, meth, args, kw, inc, amt = it
                            r = getattr(engobj, meth)(*args, **kw)
                            if inc is not None:
                                r.then_inc(self.sems[inc], amt)

                getattr(blk, e)(runner)


def build_program():
    nc = bass.Bass("TRN2", target_bir_lowering=False)

    x_in = nc.dram_tensor("x", [S, T, D], F32, kind="ExternalInput")
    w_qkv = nc.dram_tensor("w_qkv", [D, 3 * D], F32, kind="ExternalInput")
    w_out = nc.dram_tensor("w_out", [D, D], F32, kind="ExternalInput")
    b_out = nc.dram_tensor("b_out", [D], F32, kind="ExternalInput")
    w1 = nc.dram_tensor("w1", [D, INTER], F32, kind="ExternalInput")
    b1 = nc.dram_tensor("b1", [INTER], F32, kind="ExternalInput")
    w2 = nc.dram_tensor("w2", [INTER, D], F32, kind="ExternalInput")
    b2 = nc.dram_tensor("b2", [D], F32, kind="ExternalInput")
    c_ident = nc.dram_tensor("c_ident", [128, 128], F32, kind="ExternalInput")
    c_ones65 = nc.dram_tensor("c_ones65", [65, 128], F32, kind="ExternalInput")
    cb_isel = nc.dram_tensor("cb_isel", [128, 128], BF16, kind="ExternalInput")
    cb_tri01 = nc.dram_tensor("cb_tri01", [128, 128], BF16, kind="ExternalInput")
    cb_onescol = nc.dram_tensor("cb_onescol", [128, 128], BF16, kind="ExternalInput")
    cb_E = nc.dram_tensor("cb_E", [128, 8 * 128], BF16, kind="ExternalInput")
    cb_iselp = nc.dram_tensor("cb_iselp", [128, 256], BF16, kind="ExternalInput")
    cb_ZC = nc.dram_tensor("cb_ZC", [128, 32 * 32], BF16, kind="ExternalInput")
    cb_Z2C = nc.dram_tensor("cb_Z2C", [32, 16 * 128], BF16, kind="ExternalInput")
    out = nc.dram_tensor("out", [S, T, D], F32, kind="ExternalOutput")
    x1s = nc.dram_tensor("x1s", [S, NCH, 128, T], F32)
    x2s = nc.dram_tensor("x2s", [S, NCH, 128, T], F32)

    stack = contextlib.ExitStack()
    sems = {}
    for name in ("sd_s", "sd_g", "st", "sv", "sc", "sg"):
        sems[name] = stack.enter_context(nc.semaphore(name))
    pl = Plan(nc, sems)

    sb = lambda name, shape, dt: stack.enter_context(nc.sbuf_tensor(name, shape, dt))
    ident = sb("ident", [128, 128], F32)
    ones65 = sb("ones65", [65, 128], F32)
    isel = sb("isel", [128, 128], BF16)
    tri01 = sb("tri01", [128, 128], BF16)
    onescol = sb("onescol", [128, 128], BF16)
    iselp = sb("iselp", [128, 256], BF16)
    b1_sb = sb("b1sb", [128, NIC], F32)
    b2_sb = sb("b2sb", [128, NCH], F32)
    bout_sb = sb("boutsb", [128, NCH], F32)
    pb = [stack.enter_context(nc.psum_tensor("pb%d" % i, [128, 512], F32))
          for i in range(8)]
    bank_free = [None] * 8

    for dst, src in ((ident, c_ident), (ones65, c_ones65), (isel, cb_isel),
                     (tri01, cb_tri01), (onescol, cb_onescol), (iselp, cb_iselp)):
        pl.dma("sync", dst[:, :], src.ap())
    pl.op("sync", "dma_start", b1_sb[:, :],
          b1.ap().rearrange("(i p) -> p i", p=128),
          allow_slow_non_contiguous=True, inc="sd_s", inc_amt=16)
    pl.op("sync", "dma_start", b2_sb[:, :],
          b2.ap().rearrange("(i p) -> p i", p=128),
          allow_slow_non_contiguous=True, inc="sd_s", inc_amt=16)
    consts_tok = pl.op("sync", "dma_start", bout_sb[:, :],
                       b_out.ap().rearrange("(i p) -> p i", p=128),
                       allow_slow_non_contiguous=True, inc="sd_s", inc_amt=16)

    def emit_ln(src, xbf, sq, tmp, lnt, ntok, src_ready, bk_sx, bk_sq, bk_bc,
                lnrow, bcst, in_place=False, xbf_free=None, lnt_free=None):
        q = ntok * NCH
        t_xbf = pl.op("vector", "tensor_copy", xbf[:, 0:q], src[:, 0:q],
                      waits=[src_ready, consts_tok, xbf_free], inc="sv")
        if sq is not None:
            t_sq = pl.op("vector", "tensor_tensor", sq[:, 0:q], xbf[:, 0:q],
                         xbf[:, 0:q], ALU.mult, inc="sv")
            for ch in range(NCH):
                for bk, moving, tw in ((bk_sx, xbf, t_xbf), (bk_sq, sq, t_sq)):
                    w = [tw] + ([bank_free[bk]] if ch == 0 else [])
                    pl.op("tensor", "matmul", pb[bk][0:128, 0:ntok],
                          onescol[:, :], moving[:, ch * ntok:(ch + 1) * ntok],
                          start=(ch == 0), stop=(ch == NCH - 1),
                          waits=w, inc=("st" if ch == NCH - 1 else None))
        else:
            for ch in range(NCH):
                w = [t_xbf] + ([bank_free[bk_sx]] if ch == 0 else [])
                pl.op("tensor", "matmul", pb[bk_sx][0:128, 0:ntok],
                      onescol[:, :], xbf[:, ch * ntok:(ch + 1) * ntok],
                      start=(ch == 0), stop=(ch == NCH - 1),
                      waits=w, inc=("st" if ch == NCH - 1 else None))
            t_sx = pl.tok("st")
            t_sq = pl.op("vector", "tensor_tensor", xbf[:, 0:q], xbf[:, 0:q],
                         xbf[:, 0:q], ALU.mult, waits=[t_sx], inc="sv")
            for ch in range(NCH):
                w = [t_sq] + ([bank_free[bk_sq]] if ch == 0 else [])
                pl.op("tensor", "matmul", pb[bk_sq][0:128, 0:ntok],
                      onescol[:, :], xbf[:, ch * ntok:(ch + 1) * ntok],
                      start=(ch == 0), stop=(ch == NCH - 1),
                      waits=w, inc=("st" if ch == NCH - 1 else None))
        t_stats = pl.op("tensor", "engine_nop", inc="st") if False else pl.tok("st")
        mean = lnrow[0:1, 0:ntok]
        ex2 = lnrow[0:1, ntok:2 * ntok]
        ve = lnrow[0:1, 2 * ntok:3 * ntok]
        r0 = lnrow[0:1, 3 * ntok:4 * ntok]
        e1 = ex2
        pl.op("vector", "tensor_scalar", mean, pb[bk_sx][0:1, 0:ntok],
              1.0 / D, None, op0=ALU.mult, waits=[t_stats], inc="sv")
        pl.op("vector", "tensor_scalar", ex2, pb[bk_sq][0:1, 0:ntok],
              1.0 / D, None, op0=ALU.mult, inc="sv")
        pl.op("vector", "tensor_tensor", ve, mean, mean, ALU.mult, inc="sv")
        t_ve = pl.op("vector", "scalar_tensor_tensor", ve, ex2, 1e-5, ve,
                     op0=ALU.add, op1=ALU.subtract, inc="sv")
        bank_free[bk_sx] = t_ve
        bank_free[bk_sq] = t_ve
        t_sq2 = pl.op("scalar", "activation", r0, ve, AF.Sqrt, waits=[t_ve], inc="sc")
        pl.op("vector", "reciprocal", r0, r0, waits=[t_sq2], inc="sv")
        pl.op("vector", "tensor_tensor", e1, r0, r0, ALU.mult, inc="sv")
        pl.op("vector", "tensor_tensor", e1, e1, ve, ALU.mult, inc="sv")
        pl.op("vector", "tensor_scalar", e1, e1, -0.5, 1.5, op0=ALU.mult,
              op1=ALU.add, inc="sv")
        pl.op("vector", "tensor_tensor", bcst[0:1, 0:ntok], r0, e1, ALU.mult,
              inc="sv")
        t_row = pl.op("vector", "tensor_tensor", bcst[0:1, ntok:2 * ntok], mean,
                      bcst[0:1, 0:ntok], ALU.mult, inc="sv")
        if 2 * ntok <= 512:
            t_bc = pl.op("tensor", "matmul", pb[bk_bc][:, 0:2 * ntok],
                         ones65[0:1, :], bcst[0:1, 0:2 * ntok], start=True,
                         stop=True, waits=[t_row, bank_free[bk_bc]], inc="st")
            Rb = pb[bk_bc][:, 0:ntok]
            MRb = pb[bk_bc][:, ntok:2 * ntok]
        else:
            pl.op("tensor", "matmul", pb[bk_bc][:, 0:ntok],
                  ones65[0:1, :], bcst[0:1, 0:ntok], start=True, stop=True,
                  waits=[t_row, bank_free[bk_bc]], inc="st")
            t_bc = pl.op("tensor", "matmul", pb[bk_sx][:, 0:ntok],
                         ones65[0:1, :], bcst[0:1, ntok:2 * ntok], start=True,
                         stop=True, waits=[t_row, bank_free[bk_sx]], inc="st")
            Rb = pb[bk_bc][:, 0:ntok]
            MRb = pb[bk_sx][:, 0:ntok]
        last = None
        for ch in range(NCH):
            dst = src if in_place else tmp
            pl.op("vector", "tensor_tensor", dst[:, ch * ntok:(ch + 1) * ntok],
                  src[:, ch * ntok:(ch + 1) * ntok], Rb, ALU.mult,
                  waits=[t_bc, lnt_free], inc="sv")
            last = pl.op("vector", "tensor_tensor", lnt[:, ch * ntok:(ch + 1) * ntok],
                         dst[:, ch * ntok:(ch + 1) * ntok], MRb, ALU.subtract,
                         inc="sv")
        bank_free[bk_bc] = last
        if 2 * ntok > 512:
            bank_free[bk_sx] = last
        return last

    def ffn_phase(first):
        es = contextlib.ExitStack()
        sfx = "_p1" if first else "_p3"
        al = lambda n, sh, dt: es.enter_context(nc.sbuf_tensor(n + sfx, sh, dt))
        w1sb = al("w1sb", [128, NCH * INTER], BF16)
        w2sb = al("w2sb", [128, NIC * D], BF16)
        hbuf = al("hbuf", [128, NIC * FT], BF16)
        xfm0 = al("xfm0", [128, NCH * FT], F32)
        xfm1 = al("xfm1", [128, NCH * FT], F32)
        if first:
            xq0 = al("xq0", [128, D], F32)
            xq1 = al("xq1", [128, D], F32)
        else:
            xq0 = xq1 = None
        lnt0 = al("lnt0", [128, NCH * FT], BF16)
        lnt1 = al("lnt1", [128, NCH * FT], BF16)
        xbf = al("xbf_f", [128, NCH * FT], BF16)
        sqb = al("sq_f", [128, NCH * FT], BF16)
        tmp = al("tmp_f", [128, NCH * FT], F32)
        ytmp = al("ytmp", [128, FT], F32)
        lnrow = al("lnrow_f", [1, 4 * FT], F32)
        bcst = al("bcst_f", [1, 2 * FT], F32)
        if not first:
            tm0 = al("tm0", [128, D], F32)
            tm1 = al("tm1", [128, D], F32)
        else:
            tm0 = tm1 = None
        if True:
            pl.barrier()
            xfm = [xfm0, xfm1]
            xq = [xq0, xq1]
            lnts = [lnt0, lnt1]
            tms = [tm0, tm1]
            w1tok = []
            for kc in range(NCH):
                w1tok.append(pl.dma("gpsimd", w1sb[:, kc * INTER:(kc + 1) * INTER],
                                    w1.ap()[kc * 128:(kc + 1) * 128, :]))
            w2tok = []
            w2v = w2.ap().rearrange("(i p) m -> p i m", p=128)
            for g in range(4):
                w2tok.append(pl.dma(
                    "gpsimd",
                    w2sb[:, g * 8 * D:(g + 1) * 8 * D].rearrange("p (i m) -> p i m", i=8),
                    w2v[:, g * 8:(g + 1) * 8, :]))

            xq_free = [None, None]
            xfm_free = [None, None]
            lnt_ready = [None, None]
            lnt_free = [None, None]
            xbf_free = None
            hready = [None] * NIC
            tm_free = [None, None]

            def emit_lnchain(s, t):
                nonlocal xbf_free
                cur = xfm[t % 2]
                if first:
                    src_ready = None
                    for qi in range(2):
                        buf = xq[qi]
                        ld = pl.dma("sync", buf[:, :],
                                    x_in.ap()[s, t * FT + qi * 128:
                                              t * FT + (qi + 1) * 128, :],
                                    waits=[xq_free[qi]])
                        for j in range(NCH):
                            bk = 6 + (j // 4)
                            w = [ld, consts_tok] + ([bank_free[bk]] if j % 4 == 0 else [])
                            pl.op("tensor", "transpose",
                                  pb[bk][:, (j % 4) * 128:(j % 4 + 1) * 128],
                                  buf[:, j * 128:(j + 1) * 128], ident[:, :],
                                  waits=w, inc=("st" if j % 4 == 3 else None))
                            if j % 4 == 3:
                                ttr = pl.tok("st")
                                dst = cur[:, :].rearrange("p (c f) -> p c f", c=NCH)[
                                    :, j - 3:j + 1, qi * 128:(qi + 1) * 128]
                                ev = pl.op("vector", "tensor_copy", dst,
                                           pb[bk][:, 0:512].rearrange(
                                               "p (c f) -> p c f", c=4),
                                           waits=[ttr, xfm_free[t % 2]], inc="sv")
                                bank_free[bk] = ev
                                src_ready = ev
                        xq_free[qi] = pl.tok("st")
                else:
                    src_ready = pl.dma(
                        "sync", cur[:, :].rearrange("p (c f) -> p c f", c=NCH),
                        x2s.ap()[s, :, :, t * FT:(t + 1) * FT].transpose([1, 0, 2]),
                        waits=[xfm_free[t % 2]])
                lnt_ready[t % 2] = emit_ln(
                    cur, xbf, sqb, tmp, lnts[t % 2], FT, src_ready, 4, 5, 5,
                    lnrow, bcst, xbf_free=xbf_free, lnt_free=lnt_free[t % 2])
                xbf_free = pl.tok("st")

            def emit_ffn(s, t):
                cur = xfm[t % 2]
                lnt = lnts[t % 2]
                for ic in range(NIC):
                    bk = ic % 2
                    for kc in range(NCH):
                        w = [lnt_ready[t % 2], w1tok[kc]] + (
                            [bank_free[bk]] if kc == 0 else [])
                        pl.op("tensor", "matmul", pb[bk][:, 0:FT],
                              w1sb[:, kc * INTER + ic * 128:kc * INTER + (ic + 1) * 128],
                              lnt[:, kc * FT:(kc + 1) * FT],
                              start=(kc == 0), stop=(kc == NCH - 1),
                              waits=w, inc=("st" if kc == NCH - 1 else None))
                    tmm = pl.tok("st")
                    hready[ic] = pl.op("scalar", "activation",
                                       hbuf[:, ic * FT:(ic + 1) * FT],
                                       pb[bk][:, 0:FT], AF.Relu,
                                       bias=b1_sb[:, ic:ic + 1],
                                       waits=[tmm], inc="sc")
                    bank_free[bk] = hready[ic]
                lnt_free[t % 2] = pl.tok("st")
                ev = None
                for oc in range(NCH):
                    bk = 2 + oc % 2
                    for ic in range(NIC):
                        w = [hready[ic], w2tok[ic // 8]] + (
                            [bank_free[bk]] if ic == 0 else [])
                        pl.op("tensor", "matmul", pb[bk][:, 0:FT],
                              w2sb[:, ic * D + oc * 128:ic * D + (oc + 1) * 128],
                              hbuf[:, ic * FT:(ic + 1) * FT],
                              start=(ic == 0), stop=(ic == NIC - 1),
                              waits=w, inc=("st" if ic == NIC - 1 else None))
                    tmm = pl.tok("st")
                    pl.op("vector", "tensor_scalar", ytmp[:, :], pb[bk][:, 0:FT],
                          b2_sb[:, oc:oc + 1], 0.5, op0=ALU.add, op1=ALU.mult,
                          waits=[tmm], inc="sv")
                    ev = pl.op("vector", "tensor_tensor",
                               cur[:, oc * FT:(oc + 1) * FT], ytmp[:, :],
                               cur[:, oc * FT:(oc + 1) * FT], ALU.add, inc="sv")
                    bank_free[bk] = ev
                if first:
                    xfm_free[t % 2] = pl.dma(
                        "gpsimd",
                        x1s.ap()[s, :, :, t * FT:(t + 1) * FT].transpose([1, 0, 2]),
                        cur[:, :].rearrange("p (c f) -> p c f", c=NCH), waits=[ev])
                else:
                    for hf in range(2):
                        tmb = tms[hf]
                        for j in range(NCH):
                            bk = 6 + (j // 4)
                            w = [ev] + ([bank_free[bk]] if j % 4 == 0 else [])
                            pl.op("tensor", "transpose",
                                  pb[bk][:, (j % 4) * 128:(j % 4 + 1) * 128],
                                  cur[:, j * FT + hf * 128:j * FT + hf * 128 + 128],
                                  ident[:, :],
                                  waits=w, inc=("st" if j % 4 == 3 else None))
                            if j % 4 == 3:
                                ttr = pl.tok("st")
                                evt = pl.op("vector", "tensor_copy",
                                            tmb[:, (j - 3) * 128:(j + 1) * 128],
                                            pb[bk][:, 0:512],
                                            waits=[ttr, tm_free[hf]], inc="sv")
                                bank_free[bk] = evt
                        tm_free[hf] = pl.dma(
                            "gpsimd",
                            out.ap()[s, t * FT + hf * 128:t * FT + hf * 128 + 128, :],
                            tmb[:, :], waits=[pl.tok("sv")])
                    xfm_free[t % 2] = pl.tok("st")

            for s in range(S):
                for t in range(NFT + 1):
                    if t < NFT:
                        emit_lnchain(s, t)
                    if t > 0:
                        emit_ffn(s, t - 1)
            pl.flush()
        es.close()
        for i in range(8):
            bank_free[i] = None

    def attn_phase(s):
        es0 = contextlib.ExitStack()
        al0 = lambda n, sh, dt: es0.enter_context(nc.sbuf_tensor(n + "_s%d" % s, sh, dt))
        qsb = al0("qsb", [128, NCH * T], BF16)
        ksb = al0("ksb", [128, NCH * T], BF16)
        vsb = al0("vsb", [128, NTC * 1024], BF16)
        mall = al0("mall", [16, T], BF16)
        if True:
            vsb4 = vsb[:, :].rearrange("p (t h d) -> p t h d", t=NTC, h=H)
            mall_tok = None
            es1 = contextlib.ExitStack()
            al1 = lambda n, sh, dt: es1.enter_context(nc.sbuf_tensor(n + "_s%d" % s, sh, dt))
            wq = al1("wqkvsb", [128, NCH * 3 * D], BF16)
            x1a = al1("x1a", [128, NCH * QT], F32)
            lnta0 = al1("lnta0", [128, NCH * QT], BF16)
            lnta1 = al1("lnta1", [128, NCH * QT], BF16)
            sqh0 = al1("sqh0", [128, QT], BF16)
            sqh1 = al1("sqh1", [128, QT], BF16)
            nq = al1("nq", [16, T], BF16)
            nk = al1("nk", [16, T], BF16)
            kmax = al1("kmax", [16, 1], F32)
            lnrow = al1("lnrow_a", [1, 4 * QT], F32)
            bcst = al1("bcst_a", [1, 2 * QT], F32)
            Esel = al1("Esel", [128, 8 * 128], BF16)
            if True:
                pl.barrier()
                lntas = [lnta0, lnta1]
                esel_tok = pl.dma("sync", Esel[:, :], cb_E.ap())
                wqtok = []
                for kc in range(NCH):
                    wqtok.append(pl.dma("gpsimd", wq[:, kc * 3 * D:(kc + 1) * 3 * D],
                                        w_qkv.ap()[kc * 128:(kc + 1) * 128, :]))
                x1a_free = None
                lnta_ready = [None, None]
                lnta_free = [None, None]
                sqh_free = [None, None]
                sqhs = [sqh0, sqh1]

                def emit_lnchain_a(qb):
                    nonlocal x1a_free
                    src_ready = pl.dma(
                        "sync", x1a[:, :].rearrange("p (c f) -> p c f", c=NCH),
                        x1s.ap()[s, :, :, qb * QT:(qb + 1) * QT].transpose([1, 0, 2]),
                        waits=[x1a_free])
                    lnta_ready[qb % 2] = emit_ln(
                        x1a, lntas[qb % 2], None, None, lntas[qb % 2], QT,
                        src_ready, 4, 5, 5, lnrow, bcst, in_place=True,
                        xbf_free=lnta_free[qb % 2], lnt_free=lnta_free[qb % 2])
                    x1a_free = lnta_ready[qb % 2]

                def emit_qkv(qb):
                    nonlocal sqh_free
                    lnta = lntas[qb % 2]
                    for which, dstsb, nbank, scale in ((0, qsb, 6, 0.125),
                                                       (1, ksb, 7, 1.0)):
                        for mch in range(NCH):
                            bk = mch % 2
                            for kc in range(NCH):
                                w = [lnta_ready[qb % 2], wqtok[kc]] + (
                                    [bank_free[bk]] if kc == 0 else [])
                                pl.op("tensor", "matmul", pb[bk][:, 0:QT],
                                      wq[:, kc * 3 * D + which * D + mch * 128:
                                         kc * 3 * D + which * D + (mch + 1) * 128],
                                      lnta[:, kc * QT:(kc + 1) * QT],
                                      start=(kc == 0), stop=(kc == NCH - 1),
                                      waits=w, inc=("st" if kc == NCH - 1 else None))
                            tmm = pl.tok("st")
                            dslice = dstsb[:, mch * T + qb * QT:mch * T + (qb + 1) * QT]
                            if scale != 1.0:
                                tev = pl.op("vector", "tensor_scalar", dslice,
                                            pb[bk][:, 0:QT], scale, None,
                                            op0=ALU.mult, waits=[tmm], inc="sv")
                            else:
                                tev = pl.op("vector", "tensor_copy", dslice,
                                            pb[bk][:, 0:QT], waits=[tmm], inc="sv")
                            bank_free[bk] = tev
                            sq = sqhs[mch % 2]
                            tsq = pl.op("vector", "tensor_tensor", sq[:, :],
                                        dslice, dslice, ALU.mult,
                                        waits=[sqh_free[mch % 2]], inc="sv")
                            w = [tsq, esel_tok] + (
                                [bank_free[nbank]] if mch == 0 else [])
                            pl.op("tensor", "matmul", pb[nbank][0:128, 0:QT],
                                  Esel[:, mch * 128:(mch + 1) * 128], sq[:, :],
                                  start=(mch == 0), stop=(mch == NCH - 1),
                                  waits=w, inc="st")
                            sqh_free[mch % 2] = pl.tok("st")
                        tnm = pl.tok("st")
                        nsb = nq if which == 0 else nk
                        tnv = pl.op("vector", "tensor_copy",
                                    nsb[:, qb * QT:(qb + 1) * QT],
                                    pb[nbank][0:16, 0:QT], waits=[tnm], inc="sv")
                        bank_free[nbank] = tnv
                    for tc in range(4):
                        gtc = qb * 4 + tc
                        for vh in range(2):
                            bk = 2 + (tc * 2 + vh) % 2
                            for kc in range(NCH):
                                w = [lnta_ready[qb % 2], wqtok[kc]] + (
                                    [bank_free[bk]] if kc == 0 else [])
                                pl.op("tensor", "matmul", pb[bk][:, 0:QT],
                                      lnta[:, kc * QT + tc * 128:
                                           kc * QT + (tc + 1) * 128],
                                      wq[:, kc * 3 * D + 2 * D + vh * QT:
                                         kc * 3 * D + 2 * D + (vh + 1) * QT],
                                      start=(kc == 0), stop=(kc == NCH - 1),
                                      waits=w, inc=("st" if kc == NCH - 1 else None))
                            tmm = pl.tok("st")
                            tev = pl.op("vector", "tensor_copy",
                                        vsb4[:, gtc, vh * 8:(vh + 1) * 8, 0:64],
                                        pb[bk][:, 0:QT].rearrange(
                                            "p (h d) -> p h d", h=8),
                                        waits=[tmm], inc="sv")
                            bank_free[bk] = tev
                    lnta_free[qb % 2] = pl.tok("st")

                for qb in range(NQB + 1):
                    if qb < NQB:
                        emit_lnchain_a(qb)
                    if qb > 0:
                        emit_qkv(qb - 1)
                tkm = pl.op("vector", "reduce_max", kmax[:, :], nk[:, :],
                            axis=AX.X, inc="sv")
                pl.wait_only("vector", [tkm])
                tms_ = pl.op("vector", "tensor_scalar", nq[:, :], nq[:, :],
                             kmax[:, :], None, op0=ALU.mult, waits=[tkm], inc="sv")
                mall_tok = pl.op("scalar", "activation", mall[:, :], nq[:, :],
                                 AF.Sqrt, waits=[tms_], inc="sc")
                pl.flush()
            es1.close()

            es2 = contextlib.ExitStack()
            al2 = lambda n, sh, dt: es2.enter_context(nc.sbuf_tensor(n + "_s%d" % s, sh, dt))
            qst0 = al2("qst0", [128, T], BF16)
            kst0 = al2("kst0", [128, T], BF16)
            pbuf0 = al2("pbuf0", [128, 12 * QT], BF16)
            pbuf1 = al2("pbuf1", [128, NTC * QT], BF16)
            opk = al2("opk", [128, NCH * T], BF16)
            woutsb = al2("woutsb", [128, NCH * D], BF16)
            zinv0 = al2("zinv0", [32, QT], BF16)
            zinv1 = al2("zinv1", [32, QT], BF16)
            x1c0 = al2("x1c0", [128, QT], F32)
            x2st0 = al2("x2st0", [128, QT], F32)
            ZC = al2("ZC", [128, 32 * 32], BF16)
            Z2C = al2("Z2C", [32, 16 * 128], BF16)
            if True:
                pl.barrier()
                pl.dma("sync", ZC[:, :], cb_ZC.ap())
                zc_tok = pl.dma("sync", Z2C[:, :], cb_Z2C.ap())
                wotok = []
                for g in range(4):
                    wotok.append(pl.dma(
                        "gpsimd", woutsb[:, g * 2 * D:(g + 1) * 2 * D]
                        .rearrange("p (c m) -> p c m", c=2),
                        w_out.ap()[g * 256:(g + 1) * 256, :]
                        .rearrange("(c p) m -> p c m", p=128)))
                qst = [qst0]
                kst = [kst0]
                pl.op("gpsimd", "memset", qst0[64:128, :], 0.0, inc="sg")
                zpad = pl.op("gpsimd", "memset", kst0[64:128, :], -1.0, inc="sg")
                pbufs = [pbuf0, pbuf1]
                qst_free = [None]
                kst_free = [None]
                pbuf_free = [None, None]
                obank_free = [None, None]
                zfirst = [True, True]
                ptok = {}
                oev = {}
                qst_built = [None]
                kst_built = [None]
                rot = [0]

                def rot_next():
                    rot[0] = (rot[0] + 1) % 3
                    return rot[0]

                def build_staging(h):
                    mch = h // 2
                    psel = (h % 2) * 128
                    evq = evk = None
                    for blk in range(4):
                        for which in range(2):
                            src = qsb if which == 0 else ksb
                            dstt = qst[0] if which == 0 else kst[0]
                            fr = qst_free[0] if which == 0 else kst_free[0]
                            bk = rot_next()
                            pl.op("tensor", "matmul", pb[bk][0:128, 0:512],
                                  iselp[:, psel:psel + 128],
                                  src[:, mch * T + blk * 512:mch * T + (blk + 1) * 512],
                                  start=True, stop=True,
                                  waits=[consts_tok, fr, bank_free[bk]], inc="st")
                            ev = pl.op("vector", "tensor_copy",
                                       dstt[0:64, blk * 512:(blk + 1) * 512],
                                       pb[bk][0:64, 0:512],
                                       waits=[pl.tok("st")], inc="sv")
                            bank_free[bk] = ev
                            if which == 0:
                                evq = ev
                            else:
                                evk = ev
                    tdm = pl.dma("gpsimd", qst[0][64:65, :], mall[h:h + 1, :],
                                 waits=[mall_tok, qst_free[0], zpad])
                    qst_built[0] = (evq, tdm)
                    kst_built[0] = (evk, zpad)

                def emit_scores(h, qb):
                    pp = qb % 2
                    nkc = 4 * qb + 4
                    mz = [None]
                    for j in range(1, 4):
                        kc = 4 * qb + j
                        mz.append(pl.op("gpsimd", "memset",
                                        pbufs[pp][:, kc * QT:kc * QT + 128 * j],
                                        0.0, waits=[pbuf_free[pp]], inc="sg"))
                    for kc in range(nkc):
                        bk = rot_next()
                        w = list(qst_built[0]) + list(kst_built[0]) + [bank_free[bk]]
                        pl.op("tensor", "matmul", pb[bk][:, 0:QT],
                              kst[0][:, kc * 128:(kc + 1) * 128],
                              qst[0][:, qb * QT:(qb + 1) * QT],
                              start=True, stop=True, waits=w, inc="st")
                        tS = pl.tok("st")
                        j = kc - 4 * qb
                        if j >= 0:
                            texp = pl.op("scalar", "activation",
                                         pbufs[pp][:, kc * QT + 128 * j:(kc + 1) * QT],
                                         pb[bk][:, 128 * j:QT], AF.Exp,
                                         waits=[tS, pbuf_free[pp]], inc="sc")
                            ttri = pl.op("vector", "tensor_tensor",
                                         pbufs[pp][:, kc * QT + 128 * j:
                                                   kc * QT + 128 * (j + 1)],
                                         pbufs[pp][:, kc * QT + 128 * j:
                                                   kc * QT + 128 * (j + 1)],
                                         tri01[:, :], ALU.mult,
                                         waits=[texp, consts_tok], inc="sv")
                            ptok[(pp, kc)] = [texp, ttri, mz[j] if j >= 1 else None]
                        else:
                            texp = pl.op("scalar", "activation",
                                         pbufs[pp][:, kc * QT:(kc + 1) * QT],
                                         pb[bk][:, 0:QT], AF.Exp,
                                         waits=[tS, pbuf_free[pp]], inc="sc")
                            ptok[(pp, kc)] = [texp]
                        bank_free[bk] = texp

                def emit_av(h, qb):
                    pp = qb % 2
                    nkc = 4 * qb + 4
                    odd = h % 2 == 1
                    ob = 3 + qb % 2
                    zbk = 5 + (qb % 2)
                    j = 16 * (qb >> 1) + h
                    rows = slice(64, 128) if odd else slice(0, 64)
                    for kc in range(nkc):
                        w = list(ptok[(pp, kc)]) + (
                            [obank_free[qb % 2]] if kc == 0 else [])
                        vap = vsb[:, kc * 1024 + (h // 2) * 128:
                                  kc * 1024 + (h // 2) * 128 + 128]
                        pl.op("tensor", "matmul", pb[ob][0:128, 0:QT],
                              vap, pbufs[pp][:, kc * QT:(kc + 1) * QT],
                              start=(kc == 0), stop=(kc == nkc - 1),
                              waits=w, inc=("st" if kc == nkc - 1 else None))
                        wz = [zc_tok] + ([bank_free[zbk]] if zfirst[qb % 2] else [])
                        pl.op("tensor", "matmul", pb[zbk][0:32, 0:QT],
                              ZC[:, j * 32:(j + 1) * 32],
                              pbufs[pp][:, kc * QT:(kc + 1) * QT],
                              start=zfirst[qb % 2],
                              stop=(h == H - 1 and qb >= 2 and kc == nkc - 1),
                              waits=wz, inc=("st" if kc == nkc - 1 else None))
                        zfirst[qb % 2] = False
                    tav = pl.tok("st")
                    pbuf_free[pp] = tav
                    osl = opk[rows, (h // 2) * T + qb * QT:
                              (h // 2) * T + (qb + 1) * QT]
                    tOe = pl.op("vector", "tensor_copy", osl,
                                pb[ob][rows, 0:QT], waits=[tav], inc="sv")
                    obank_free[qb % 2] = tOe
                    oev[(h, qb)] = tOe

                build_staging(0)
                prev_av = None
                for h in range(H):
                    for qb in range(NQB):
                        emit_scores(h, qb)
                        if qb == NQB - 1:
                            qst_free[0] = pl.tok("st")
                            kst_free[0] = pl.tok("st")
                            if h + 1 < H:
                                build_staging(h + 1)
                        if prev_av is not None:
                            emit_av(*prev_av)
                        prev_av = (h, qb)
                emit_av(*prev_av)

                t_zstop = pl.tok("st")
                zinvs = [zinv0, zinv1]
                trec = [None, None]
                for par in range(2):
                    trec[par] = pl.op("vector", "reciprocal", zinvs[par][:, :],
                                      pb[5 + par][0:32, 0:QT],
                                      waits=[t_zstop], inc="sv")
                    bank_free[5 + par] = trec[par]
                tnorm = {}
                for qb in range(NQB):
                    for kc in range(NCH):
                        bk = (qb * NCH + kc) % 2
                        slot = 8 * (qb >> 1) + kc
                        pl.op("tensor", "matmul", pb[bk][:, 0:QT],
                              Z2C[:, slot * 128:(slot + 1) * 128],
                              zinvs[qb % 2][:, :], start=True, stop=True,
                              waits=[trec[qb % 2], bank_free[bk], zc_tok],
                              inc="st")
                        osl = opk[:, kc * T + qb * QT:kc * T + (qb + 1) * QT]
                        tn = pl.op("vector", "tensor_tensor", osl, osl,
                                   pb[bk][:, 0:QT], ALU.mult,
                                   waits=[pl.tok("st"), oev[(2 * kc, qb)],
                                          oev[(2 * kc + 1, qb)]], inc="sv")
                        bank_free[bk] = tn
                        tnorm[(qb, kc)] = tn

                x1c = [x1c0]
                x2st = [x2st0]
                x1c_free = [None]
                x2st_free = [None]
                for qb in range(NQB):
                    for oc in range(NCH):
                        r = 0
                        tld = pl.dma("sync", x1c[r][:, :],
                                     x1s.ap()[s, oc, :, qb * QT:(qb + 1) * QT],
                                     waits=[x1c_free[r]])
                        bk = 7
                        for kc in range(NCH):
                            w = [wotok[kc // 2], tnorm[(qb, kc)]] + (
                                [bank_free[bk]] if kc == 0 else [])
                            pl.op("tensor", "matmul", pb[bk][:, 0:QT],
                                  woutsb[:, kc * D + oc * 128:kc * D + (oc + 1) * 128],
                                  opk[:, kc * T + qb * QT:kc * T + (qb + 1) * QT],
                                  start=(kc == 0), stop=(kc == NCH - 1),
                                  waits=w, inc=("st" if kc == NCH - 1 else None))
                        tmm = pl.tok("st")
                        tev = pl.op("vector", "scalar_tensor_tensor", x2st[r][:, :],
                                    pb[bk][:, 0:QT], bout_sb[:, oc:oc + 1],
                                    x1c[r][:, :], op0=ALU.add, op1=ALU.add,
                                    waits=[tmm, tld, x2st_free[r], consts_tok],
                                    inc="sv")
                        bank_free[bk] = tev
                        x1c_free[r] = tev
                        x2st_free[r] = pl.dma(
                            "gpsimd", x2s.ap()[s, oc, :, qb * QT:(qb + 1) * QT],
                            x2st[r][:, :], waits=[tev])
                pl.flush()
            es2.close()
        es0.close()
        for i in range(8):
            bank_free[i] = None

    ffn_phase(first=True)
    for s in range(S):
        attn_phase(s)
    ffn_phase(first=False)
    pl.barrier()
    pl.flush()
    stack.close()
    return nc


_NC_CACHE = {}


def _consts():
    ident = np.eye(128, dtype=np.float32)
    ones65 = np.zeros((65, 128), dtype=np.float32)
    ones65[0, :] = 1.0
    ones65[64, :] = 1.0
    isel = np.eye(128).astype(ml_dtypes.bfloat16)
    tri01 = np.triu(np.ones((128, 128))).astype(ml_dtypes.bfloat16)
    onescol = np.zeros((128, 128), dtype=ml_dtypes.bfloat16)
    onescol[:, 0] = 1.0
    E = np.zeros((128, 8 * 128), dtype=np.float32)
    for mch in range(8):
        for p in range(128):
            E[p, mch * 128 + 2 * mch + (p // 64)] = 1.0
    iselp = np.zeros((128, 256), dtype=np.float32)
    iselp[0:64, 0:64] = np.eye(64)
    iselp[64:128, 128:192] = np.eye(64)
    ZCm = np.zeros((128, 32 * 32), dtype=np.float32)
    for j in range(32):
        ZCm[:, j * 32 + j] = 1.0
    Z2Cm = np.zeros((32, 16 * 128), dtype=np.float32)
    for slot in range(16):
        for i in range(128):
            Z2Cm[2 * slot + i // 64, slot * 128 + i] = 1.0
    return (ident, ones65, isel, tri01, onescol, E.astype(ml_dtypes.bfloat16),
            ZCm.astype(ml_dtypes.bfloat16), Z2Cm.astype(ml_dtypes.bfloat16),
            iselp.astype(ml_dtypes.bfloat16))


def kernel(L0_attn_x, L0_attn_w_qkv, L0_attn_w_out, L0_attn_b_out,
           L0_ffn_W1, L0_ffn_b1, L0_ffn_W2, L0_ffn_b2):
    x = np.ascontiguousarray(np.asarray(L0_attn_x, dtype=np.float32))
    ident, ones65, isel, tri01, onescol, E, ZCm, Z2Cm, iselp_c = _consts()
    common = {
        "w_qkv": np.ascontiguousarray(np.asarray(L0_attn_w_qkv, dtype=np.float32)),
        "w_out": np.ascontiguousarray(np.asarray(L0_attn_w_out, dtype=np.float32)),
        "b_out": np.ascontiguousarray(np.asarray(L0_attn_b_out, dtype=np.float32)),
        "w1": np.ascontiguousarray(np.asarray(L0_ffn_W1, dtype=np.float32)),
        "b1": np.ascontiguousarray(np.asarray(L0_ffn_b1, dtype=np.float32)),
        "w2": np.ascontiguousarray(np.asarray(L0_ffn_W2, dtype=np.float32)),
        "b2": np.ascontiguousarray(np.asarray(L0_ffn_b2, dtype=np.float32)),
        "c_ident": ident, "c_ones65": ones65, "cb_isel": isel,
        "cb_tri01": tri01, "cb_onescol": onescol, "cb_E": E,
        "cb_ZC": ZCm, "cb_Z2C": Z2Cm, "cb_iselp": iselp_c,
    }
    if "nc" not in _NC_CACHE:
        _NC_CACHE["nc"] = build_program()
    nc = _NC_CACHE["nc"]
    in_maps = []
    for c in range(NCORES):
        m = dict(common)
        m["x"] = np.ascontiguousarray(x[c * S:(c + 1) * S])
        in_maps.append(m)
    res = run_bass_kernel_spmd(nc, in_maps, core_ids=list(range(NCORES)))
    return np.concatenate([np.asarray(r["out"]) for r in res.results], axis=0)


# version 06
# speedup vs baseline: 1.0628x; 1.0628x over previous
import contextlib

import numpy as np
import ml_dtypes
import concourse.bass as bass
import concourse.mybir as mybir
from concourse.bass_utils import run_bass_kernel_spmd

F32 = mybir.dt.float32
BF16 = mybir.dt.bfloat16
AF = mybir.ActivationFunctionType
ALU = mybir.AluOpType
AX = mybir.AxisListType

NCORES = 8
S = 2
T = 2048
D = 1024
H = 16
INTER = 4096
NCH = D // 128
NIC = INTER // 128
FT = 256
NFT = T // FT
QT = 512
NQB = T // QT
NTC = T // 128
VW = H * 65

ENGS = ("sync", "tensor", "vector", "scalar", "gpsimd")


class Plan:

    def __init__(self, nc, sems):
        self.nc = nc
        self.sems = sems
        self.q = {e: [] for e in ENGS}
        self.cnt = {name: 0 for name in sems}
        self.lastwait = {e: {} for e in ENGS}

    def op(self, eng, meth, *args, waits=(), inc=None, inc_amt=1, **kw):
        self.wait_only(eng, waits)
        tok = None
        if inc is not None:
            self.cnt[inc] += inc_amt
            tok = (inc, self.cnt[inc])
        self.q[eng].append(("c", meth, args, kw, inc, inc_amt))
        return tok

    def dma(self, eng, out, in_, waits=()):
        sname = "sd_s" if eng == "sync" else "sd_g"
        return self.op(eng, "dma_start", out, in_, waits=waits, inc=sname, inc_amt=16)

    def wait_only(self, eng, waits):
        lw = self.lastwait[eng]
        for tok in waits:
            if tok is None:
                continue
            sname, val = tok
            if val <= 0 or lw.get(sname, -1) >= val:
                continue
            assert val <= self.cnt[sname], (eng, sname, val, self.cnt[sname])
            lw[sname] = val
            self.q[eng].append(("w", sname, val))

    def tok(self, sname):
        return (sname, self.cnt[sname])

    def barrier(self):
        toks = [(n, c) for n, c in self.cnt.items() if c > 0]
        for e in ENGS:
            self.wait_only(e, toks)

    def flush(self):
        qs = self.q
        self.q = {e: [] for e in ENGS}
        if not any(qs[e] for e in ENGS):
            return
        with self.nc.allow_low_precision(
                reason="bf16 operands / fp32 accumulation per problem tolerance"), \
                self.nc.Block() as blk:
            for e in ENGS:
                if not qs[e]:
                    continue

                def runner(engobj, items=qs[e]):
                    for it in items:
                        if it[0] == "w":
                            engobj.wait_ge(self.sems[it[1]], it[2])
                        else:
                            _, meth, args, kw, inc, amt = it
                            r = getattr(engobj, meth)(*args, **kw)
                            if inc is not None:
                                r.then_inc(self.sems[inc], amt)

                getattr(blk, e)(runner)


def build_program():
    nc = bass.Bass("TRN2", target_bir_lowering=False)

    x_in = nc.dram_tensor("x", [S, T, D], F32, kind="ExternalInput")
    w_qkv = nc.dram_tensor("w_qkv", [D, 3 * D], F32, kind="ExternalInput")
    w_out = nc.dram_tensor("w_out", [D, D], F32, kind="ExternalInput")
    b_out = nc.dram_tensor("b_out", [D], F32, kind="ExternalInput")
    w1 = nc.dram_tensor("w1", [D, INTER], F32, kind="ExternalInput")
    b1 = nc.dram_tensor("b1", [INTER], F32, kind="ExternalInput")
    w2 = nc.dram_tensor("w2", [INTER, D], F32, kind="ExternalInput")
    b2 = nc.dram_tensor("b2", [D], F32, kind="ExternalInput")
    c_ident = nc.dram_tensor("c_ident", [128, 128], F32, kind="ExternalInput")
    c_ones65 = nc.dram_tensor("c_ones65", [65, 128], F32, kind="ExternalInput")
    cb_isel = nc.dram_tensor("cb_isel", [128, 128], BF16, kind="ExternalInput")
    cb_tri01 = nc.dram_tensor("cb_tri01", [128, 128], BF16, kind="ExternalInput")
    cb_onescol = nc.dram_tensor("cb_onescol", [128, 128], BF16, kind="ExternalInput")
    cb_E = nc.dram_tensor("cb_E", [128, 8 * 128], BF16, kind="ExternalInput")
    cb_iselp = nc.dram_tensor("cb_iselp", [128, 256], BF16, kind="ExternalInput")
    cb_ZC = nc.dram_tensor("cb_ZC", [128, 32 * 32], BF16, kind="ExternalInput")
    cb_Z2C = nc.dram_tensor("cb_Z2C", [32, 16 * 128], BF16, kind="ExternalInput")
    out = nc.dram_tensor("out", [S, T, D], F32, kind="ExternalOutput")
    x1s = nc.dram_tensor("x1s", [S, NCH, 128, T], F32)
    x2s = nc.dram_tensor("x2s", [S, NCH, 128, T], F32)

    stack = contextlib.ExitStack()
    sems = {}
    for name in ("sd_s", "sd_g", "st", "sv", "sc", "sg"):
        sems[name] = stack.enter_context(nc.semaphore(name))
    pl = Plan(nc, sems)

    sb = lambda name, shape, dt: stack.enter_context(nc.sbuf_tensor(name, shape, dt))
    ident = sb("ident", [128, 128], F32)
    ones65 = sb("ones65", [65, 128], F32)
    isel = sb("isel", [128, 128], BF16)
    tri01 = sb("tri01", [128, 128], BF16)
    onescol = sb("onescol", [128, 128], BF16)
    iselp = sb("iselp", [128, 256], BF16)
    b1_sb = sb("b1sb", [128, NIC], F32)
    b2_sb = sb("b2sb", [128, NCH], F32)
    bout_sb = sb("boutsb", [128, NCH], F32)
    pb = [stack.enter_context(nc.psum_tensor("pb%d" % i, [128, 512], F32))
          for i in range(8)]
    bank_free = [None] * 8

    for dst, src in ((ident, c_ident), (ones65, c_ones65), (isel, cb_isel),
                     (tri01, cb_tri01), (onescol, cb_onescol), (iselp, cb_iselp)):
        pl.dma("sync", dst[:, :], src.ap())
    pl.op("sync", "dma_start", b1_sb[:, :],
          b1.ap().rearrange("(i p) -> p i", p=128),
          allow_slow_non_contiguous=True, inc="sd_s", inc_amt=16)
    pl.op("sync", "dma_start", b2_sb[:, :],
          b2.ap().rearrange("(i p) -> p i", p=128),
          allow_slow_non_contiguous=True, inc="sd_s", inc_amt=16)
    consts_tok = pl.op("sync", "dma_start", bout_sb[:, :],
                       b_out.ap().rearrange("(i p) -> p i", p=128),
                       allow_slow_non_contiguous=True, inc="sd_s", inc_amt=16)

    def emit_ln(src, xbf, sq, tmp, lnt, ntok, src_ready, bk_sx, bk_sq, bk_bc,
                lnrow, bcst, in_place=False, xbf_free=None, lnt_free=None):
        q = ntok * NCH
        t_xbf = pl.op("vector", "tensor_copy", xbf[:, 0:q], src[:, 0:q],
                      waits=[src_ready, consts_tok, xbf_free], inc="sv")
        if sq is not None:
            t_sq = pl.op("vector", "tensor_tensor", sq[:, 0:q], xbf[:, 0:q],
                         xbf[:, 0:q], ALU.mult, inc="sv")
            for ch in range(NCH):
                for bk, moving, tw in ((bk_sx, xbf, t_xbf), (bk_sq, sq, t_sq)):
                    w = [tw] + ([bank_free[bk]] if ch == 0 else [])
                    pl.op("tensor", "matmul", pb[bk][0:128, 0:ntok],
                          onescol[:, :], moving[:, ch * ntok:(ch + 1) * ntok],
                          start=(ch == 0), stop=(ch == NCH - 1),
                          waits=w, inc=("st" if ch == NCH - 1 else None))
        else:
            for ch in range(NCH):
                w = [t_xbf] + ([bank_free[bk_sx]] if ch == 0 else [])
                pl.op("tensor", "matmul", pb[bk_sx][0:128, 0:ntok],
                      onescol[:, :], xbf[:, ch * ntok:(ch + 1) * ntok],
                      start=(ch == 0), stop=(ch == NCH - 1),
                      waits=w, inc=("st" if ch == NCH - 1 else None))
            t_sx = pl.tok("st")
            t_sq = pl.op("vector", "tensor_tensor", xbf[:, 0:q], xbf[:, 0:q],
                         xbf[:, 0:q], ALU.mult, waits=[t_sx], inc="sv")
            for ch in range(NCH):
                w = [t_sq] + ([bank_free[bk_sq]] if ch == 0 else [])
                pl.op("tensor", "matmul", pb[bk_sq][0:128, 0:ntok],
                      onescol[:, :], xbf[:, ch * ntok:(ch + 1) * ntok],
                      start=(ch == 0), stop=(ch == NCH - 1),
                      waits=w, inc=("st" if ch == NCH - 1 else None))
        t_stats = pl.op("tensor", "engine_nop", inc="st") if False else pl.tok("st")
        mean = lnrow[0:1, 0:ntok]
        ex2 = lnrow[0:1, ntok:2 * ntok]
        ve = lnrow[0:1, 2 * ntok:3 * ntok]
        r0 = lnrow[0:1, 3 * ntok:4 * ntok]
        e1 = ex2
        pl.op("vector", "tensor_scalar", mean, pb[bk_sx][0:1, 0:ntok],
              1.0 / D, None, op0=ALU.mult, waits=[t_stats], inc="sv")
        pl.op("vector", "tensor_scalar", ex2, pb[bk_sq][0:1, 0:ntok],
              1.0 / D, None, op0=ALU.mult, inc="sv")
        pl.op("vector", "tensor_tensor", ve, mean, mean, ALU.mult, inc="sv")
        t_ve = pl.op("vector", "scalar_tensor_tensor", ve, ex2, 1e-5, ve,
                     op0=ALU.add, op1=ALU.subtract, inc="sv")
        bank_free[bk_sx] = t_ve
        bank_free[bk_sq] = t_ve
        t_sq2 = pl.op("scalar", "activation", r0, ve, AF.Sqrt, waits=[t_ve], inc="sc")
        pl.op("vector", "reciprocal", r0, r0, waits=[t_sq2], inc="sv")
        pl.op("vector", "tensor_tensor", e1, r0, r0, ALU.mult, inc="sv")
        pl.op("vector", "tensor_tensor", e1, e1, ve, ALU.mult, inc="sv")
        pl.op("vector", "tensor_scalar", e1, e1, -0.5, 1.5, op0=ALU.mult,
              op1=ALU.add, inc="sv")
        pl.op("vector", "tensor_tensor", bcst[0:1, 0:ntok], r0, e1, ALU.mult,
              inc="sv")
        t_row = pl.op("vector", "tensor_tensor", bcst[0:1, ntok:2 * ntok], mean,
                      bcst[0:1, 0:ntok], ALU.mult, inc="sv")
        if 2 * ntok <= 512:
            t_bc = pl.op("tensor", "matmul", pb[bk_bc][:, 0:2 * ntok],
                         ones65[0:1, :], bcst[0:1, 0:2 * ntok], start=True,
                         stop=True, waits=[t_row, bank_free[bk_bc]], inc="st")
            Rb = pb[bk_bc][:, 0:ntok]
            MRb = pb[bk_bc][:, ntok:2 * ntok]
        else:
            pl.op("tensor", "matmul", pb[bk_bc][:, 0:ntok],
                  ones65[0:1, :], bcst[0:1, 0:ntok], start=True, stop=True,
                  waits=[t_row, bank_free[bk_bc]], inc="st")
            t_bc = pl.op("tensor", "matmul", pb[bk_sx][:, 0:ntok],
                         ones65[0:1, :], bcst[0:1, ntok:2 * ntok], start=True,
                         stop=True, waits=[t_row, bank_free[bk_sx]], inc="st")
            Rb = pb[bk_bc][:, 0:ntok]
            MRb = pb[bk_sx][:, 0:ntok]
        last = None
        for ch in range(NCH):
            dst = src if in_place else tmp
            pl.op("vector", "tensor_tensor", dst[:, ch * ntok:(ch + 1) * ntok],
                  src[:, ch * ntok:(ch + 1) * ntok], Rb, ALU.mult,
                  waits=[t_bc, lnt_free], inc="sv")
            last = pl.op("vector", "tensor_tensor", lnt[:, ch * ntok:(ch + 1) * ntok],
                         dst[:, ch * ntok:(ch + 1) * ntok], MRb, ALU.subtract,
                         inc="sv")
        bank_free[bk_bc] = last
        if 2 * ntok > 512:
            bank_free[bk_sx] = last
        return last

    def ffn_phase(first):
        es = contextlib.ExitStack()
        sfx = "_p1" if first else "_p3"
        al = lambda n, sh, dt: es.enter_context(nc.sbuf_tensor(n + sfx, sh, dt))
        w1sb = al("w1sb", [128, NCH * INTER], BF16)
        w2sb = al("w2sb", [128, NIC * D], BF16)
        hbuf = al("hbuf", [128, NIC * FT], BF16)
        xfm0 = al("xfm0", [128, NCH * FT], F32)
        xfm1 = al("xfm1", [128, NCH * FT], F32)
        if first:
            xq0 = al("xq0", [128, D], F32)
            xq1 = al("xq1", [128, D], F32)
        else:
            xq0 = xq1 = None
        lnt0 = al("lnt0", [128, NCH * FT], BF16)
        lnt1 = al("lnt1", [128, NCH * FT], BF16)
        xbf = al("xbf_f", [128, NCH * FT], BF16)
        sqb = al("sq_f", [128, NCH * FT], BF16)
        tmp = al("tmp_f", [128, NCH * FT], F32)
        ytmp = al("ytmp", [128, FT], F32)
        lnrow = al("lnrow_f", [1, 4 * FT], F32)
        bcst = al("bcst_f", [1, 2 * FT], F32)
        if not first:
            tm0 = al("tm0", [128, D], F32)
            tm1 = al("tm1", [128, D], F32)
        else:
            tm0 = tm1 = None
        if True:
            pl.barrier()
            xfm = [xfm0, xfm1]
            xq = [xq0, xq1]
            lnts = [lnt0, lnt1]
            tms = [tm0, tm1]
            w1tok = []
            for kc in range(NCH):
                w1tok.append(pl.dma("gpsimd", w1sb[:, kc * INTER:(kc + 1) * INTER],
                                    w1.ap()[kc * 128:(kc + 1) * 128, :]))
            w2tok = []
            w2v = w2.ap().rearrange("(i p) m -> p i m", p=128)
            for g in range(4):
                w2tok.append(pl.dma(
                    "gpsimd",
                    w2sb[:, g * 8 * D:(g + 1) * 8 * D].rearrange("p (i m) -> p i m", i=8),
                    w2v[:, g * 8:(g + 1) * 8, :]))

            xq_free = [None, None]
            xfm_free = [None, None]
            lnt_ready = [None, None]
            lnt_free = [None, None]
            xbf_free = None
            hready = [None] * NIC
            tm_free = [None, None]

            def emit_lnchain(s, t):
                nonlocal xbf_free
                cur = xfm[t % 2]
                if first:
                    src_ready = None
                    for qi in range(2):
                        buf = xq[qi]
                        ld = pl.dma("sync", buf[:, :],
                                    x_in.ap()[s, t * FT + qi * 128:
                                              t * FT + (qi + 1) * 128, :],
                                    waits=[xq_free[qi]])
                        for j in range(NCH):
                            bk = 6 + (j // 4)
                            w = [ld, consts_tok] + ([bank_free[bk]] if j % 4 == 0 else [])
                            pl.op("tensor", "transpose",
                                  pb[bk][:, (j % 4) * 128:(j % 4 + 1) * 128],
                                  buf[:, j * 128:(j + 1) * 128], ident[:, :],
                                  waits=w, inc=("st" if j % 4 == 3 else None))
                            if j % 4 == 3:
                                ttr = pl.tok("st")
                                dst = cur[:, :].rearrange("p (c f) -> p c f", c=NCH)[
                                    :, j - 3:j + 1, qi * 128:(qi + 1) * 128]
                                ev = pl.op("vector", "tensor_copy", dst,
                                           pb[bk][:, 0:512].rearrange(
                                               "p (c f) -> p c f", c=4),
                                           waits=[ttr, xfm_free[t % 2]], inc="sv")
                                bank_free[bk] = ev
                                src_ready = ev
                        xq_free[qi] = pl.tok("st")
                else:
                    src_ready = pl.dma(
                        "sync", cur[:, :].rearrange("p (c f) -> p c f", c=NCH),
                        x2s.ap()[s, :, :, t * FT:(t + 1) * FT].transpose([1, 0, 2]),
                        waits=[xfm_free[t % 2]])
                lnt_ready[t % 2] = emit_ln(
                    cur, xbf, sqb, tmp, lnts[t % 2], FT, src_ready, 4, 5, 5,
                    lnrow, bcst, xbf_free=xbf_free, lnt_free=lnt_free[t % 2])
                xbf_free = pl.tok("st")

            def emit_ffn(s, t):
                cur = xfm[t % 2]
                lnt = lnts[t % 2]
                for ic in range(NIC):
                    bk = ic % 2
                    for kc in range(NCH):
                        w = [lnt_ready[t % 2], w1tok[kc]] + (
                            [bank_free[bk]] if kc == 0 else [])
                        pl.op("tensor", "matmul", pb[bk][:, 0:FT],
                              w1sb[:, kc * INTER + ic * 128:kc * INTER + (ic + 1) * 128],
                              lnt[:, kc * FT:(kc + 1) * FT],
                              start=(kc == 0), stop=(kc == NCH - 1),
                              waits=w, inc=("st" if kc == NCH - 1 else None))
                    tmm = pl.tok("st")
                    hready[ic] = pl.op("scalar", "activation",
                                       hbuf[:, ic * FT:(ic + 1) * FT],
                                       pb[bk][:, 0:FT], AF.Relu,
                                       bias=b1_sb[:, ic:ic + 1],
                                       waits=[tmm], inc="sc")
                    bank_free[bk] = hready[ic]
                lnt_free[t % 2] = pl.tok("st")
                ev = None
                for oc in range(NCH):
                    bk = 2 + oc % 2
                    for ic in range(NIC):
                        w = [hready[ic], w2tok[ic // 8]] + (
                            [bank_free[bk]] if ic == 0 else [])
                        pl.op("tensor", "matmul", pb[bk][:, 0:FT],
                              w2sb[:, ic * D + oc * 128:ic * D + (oc + 1) * 128],
                              hbuf[:, ic * FT:(ic + 1) * FT],
                              start=(ic == 0), stop=(ic == NIC - 1),
                              waits=w, inc=("st" if ic == NIC - 1 else None))
                    tmm = pl.tok("st")
                    pl.op("vector", "tensor_scalar", ytmp[:, :], pb[bk][:, 0:FT],
                          b2_sb[:, oc:oc + 1], 0.5, op0=ALU.add, op1=ALU.mult,
                          waits=[tmm], inc="sv")
                    ev = pl.op("vector", "tensor_tensor",
                               cur[:, oc * FT:(oc + 1) * FT], ytmp[:, :],
                               cur[:, oc * FT:(oc + 1) * FT], ALU.add, inc="sv")
                    bank_free[bk] = ev
                if first:
                    xfm_free[t % 2] = pl.dma(
                        "gpsimd",
                        x1s.ap()[s, :, :, t * FT:(t + 1) * FT].transpose([1, 0, 2]),
                        cur[:, :].rearrange("p (c f) -> p c f", c=NCH), waits=[ev])
                else:
                    for hf in range(2):
                        tmb = tms[hf]
                        for j in range(NCH):
                            bk = 6 + (j // 4)
                            w = [ev] + ([bank_free[bk]] if j % 4 == 0 else [])
                            pl.op("tensor", "transpose",
                                  pb[bk][:, (j % 4) * 128:(j % 4 + 1) * 128],
                                  cur[:, j * FT + hf * 128:j * FT + hf * 128 + 128],
                                  ident[:, :],
                                  waits=w, inc=("st" if j % 4 == 3 else None))
                            if j % 4 == 3:
                                ttr = pl.tok("st")
                                evt = pl.op("vector", "tensor_copy",
                                            tmb[:, (j - 3) * 128:(j + 1) * 128],
                                            pb[bk][:, 0:512],
                                            waits=[ttr, tm_free[hf]], inc="sv")
                                bank_free[bk] = evt
                        tm_free[hf] = pl.dma(
                            "gpsimd",
                            out.ap()[s, t * FT + hf * 128:t * FT + hf * 128 + 128, :],
                            tmb[:, :], waits=[pl.tok("sv")])
                    xfm_free[t % 2] = pl.tok("st")

            for s in range(S):
                for t in range(NFT + 1):
                    if t < NFT:
                        emit_lnchain(s, t)
                    if t > 0:
                        emit_ffn(s, t - 1)
            pl.flush()
        es.close()
        for i in range(8):
            bank_free[i] = None

    def attn_phase(s):
        es0 = contextlib.ExitStack()
        al0 = lambda n, sh, dt: es0.enter_context(nc.sbuf_tensor(n + "_s%d" % s, sh, dt))
        qsb = al0("qsb", [128, NCH * T], BF16)
        ksb = al0("ksb", [128, NCH * T], BF16)
        vsb = al0("vsb", [128, NTC * 1024], BF16)
        mall = al0("mall", [16, T], BF16)
        if True:
            vsb4 = vsb[:, :].rearrange("p (t h d) -> p t h d", t=NTC, h=H)
            mall_tok = None
            es1 = contextlib.ExitStack()
            al1 = lambda n, sh, dt: es1.enter_context(nc.sbuf_tensor(n + "_s%d" % s, sh, dt))
            wq = al1("wqkvsb", [128, NCH * 3 * D], BF16)
            x1a = al1("x1a", [128, NCH * QT], F32)
            lnta0 = al1("lnta0", [128, NCH * QT], BF16)
            lnta1 = al1("lnta1", [128, NCH * QT], BF16)
            sqh0 = al1("sqh0", [128, QT], BF16)
            sqh1 = al1("sqh1", [128, QT], BF16)
            nq = al1("nq", [16, T], BF16)
            nk = al1("nk", [16, T], BF16)
            kmax = al1("kmax", [16, 1], F32)
            lnrow = al1("lnrow_a", [1, 4 * QT], F32)
            bcst = al1("bcst_a", [1, 2 * QT], F32)
            Esel = al1("Esel", [128, 8 * 128], BF16)
            if True:
                pl.barrier()
                lntas = [lnta0, lnta1]
                esel_tok = pl.dma("sync", Esel[:, :], cb_E.ap())
                wqtok = []
                for kc in range(NCH):
                    wqtok.append(pl.dma("gpsimd", wq[:, kc * 3 * D:(kc + 1) * 3 * D],
                                        w_qkv.ap()[kc * 128:(kc + 1) * 128, :]))
                x1a_free = None
                lnta_ready = [None, None]
                lnta_free = [None, None]
                sqh_free = [None, None]
                sqhs = [sqh0, sqh1]

                def emit_lnchain_a(qb):
                    nonlocal x1a_free
                    src_ready = pl.dma(
                        "sync", x1a[:, :].rearrange("p (c f) -> p c f", c=NCH),
                        x1s.ap()[s, :, :, qb * QT:(qb + 1) * QT].transpose([1, 0, 2]),
                        waits=[x1a_free])
                    lnta_ready[qb % 2] = emit_ln(
                        x1a, lntas[qb % 2], None, None, lntas[qb % 2], QT,
                        src_ready, 4, 5, 5, lnrow, bcst, in_place=True,
                        xbf_free=lnta_free[qb % 2], lnt_free=lnta_free[qb % 2])
                    x1a_free = lnta_ready[qb % 2]

                def emit_qkv(qb):
                    nonlocal sqh_free
                    lnta = lntas[qb % 2]
                    for which, dstsb, nbank, scale in ((0, qsb, 6, 0.125),
                                                       (1, ksb, 7, 1.0)):
                        for mch in range(NCH):
                            bk = mch % 2
                            for kc in range(NCH):
                                w = [lnta_ready[qb % 2], wqtok[kc]] + (
                                    [bank_free[bk]] if kc == 0 else [])
                                pl.op("tensor", "matmul", pb[bk][:, 0:QT],
                                      wq[:, kc * 3 * D + which * D + mch * 128:
                                         kc * 3 * D + which * D + (mch + 1) * 128],
                                      lnta[:, kc * QT:(kc + 1) * QT],
                                      start=(kc == 0), stop=(kc == NCH - 1),
                                      waits=w, inc=("st" if kc == NCH - 1 else None))
                            tmm = pl.tok("st")
                            dslice = dstsb[:, mch * T + qb * QT:mch * T + (qb + 1) * QT]
                            if scale != 1.0:
                                tev = pl.op("vector", "tensor_scalar", dslice,
                                            pb[bk][:, 0:QT], scale, None,
                                            op0=ALU.mult, waits=[tmm], inc="sv")
                            else:
                                tev = pl.op("vector", "tensor_copy", dslice,
                                            pb[bk][:, 0:QT], waits=[tmm], inc="sv")
                            bank_free[bk] = tev
                            sq = sqhs[mch % 2]
                            tsq = pl.op("vector", "tensor_tensor", sq[:, :],
                                        dslice, dslice, ALU.mult,
                                        waits=[sqh_free[mch % 2]], inc="sv")
                            w = [tsq, esel_tok] + (
                                [bank_free[nbank]] if mch == 0 else [])
                            pl.op("tensor", "matmul", pb[nbank][0:128, 0:QT],
                                  Esel[:, mch * 128:(mch + 1) * 128], sq[:, :],
                                  start=(mch == 0), stop=(mch == NCH - 1),
                                  waits=w, inc="st")
                            sqh_free[mch % 2] = pl.tok("st")
                        tnm = pl.tok("st")
                        nsb = nq if which == 0 else nk
                        tnv = pl.op("vector", "tensor_copy",
                                    nsb[:, qb * QT:(qb + 1) * QT],
                                    pb[nbank][0:16, 0:QT], waits=[tnm], inc="sv")
                        bank_free[nbank] = tnv
                    for tc in range(4):
                        gtc = qb * 4 + tc
                        for vh in range(2):
                            bk = 2 + (tc * 2 + vh) % 2
                            for kc in range(NCH):
                                w = [lnta_ready[qb % 2], wqtok[kc]] + (
                                    [bank_free[bk]] if kc == 0 else [])
                                pl.op("tensor", "matmul", pb[bk][:, 0:QT],
                                      lnta[:, kc * QT + tc * 128:
                                           kc * QT + (tc + 1) * 128],
                                      wq[:, kc * 3 * D + 2 * D + vh * QT:
                                         kc * 3 * D + 2 * D + (vh + 1) * QT],
                                      start=(kc == 0), stop=(kc == NCH - 1),
                                      waits=w, inc=("st" if kc == NCH - 1 else None))
                            tmm = pl.tok("st")
                            tev = pl.op("vector", "tensor_copy",
                                        vsb4[:, gtc, vh * 8:(vh + 1) * 8, 0:64],
                                        pb[bk][:, 0:QT].rearrange(
                                            "p (h d) -> p h d", h=8),
                                        waits=[tmm], inc="sv")
                            bank_free[bk] = tev
                    lnta_free[qb % 2] = pl.tok("st")

                for qb in range(NQB + 1):
                    if qb < NQB:
                        emit_lnchain_a(qb)
                    if qb > 0:
                        emit_qkv(qb - 1)
                tkm = pl.op("vector", "reduce_max", kmax[:, :], nk[:, :],
                            axis=AX.X, inc="sv")
                pl.wait_only("vector", [tkm])
                tms_ = pl.op("vector", "tensor_scalar", nq[:, :], nq[:, :],
                             kmax[:, :], None, op0=ALU.mult, waits=[tkm], inc="sv")
                mall_tok = pl.op("scalar", "activation", mall[:, :], nq[:, :],
                                 AF.Sqrt, waits=[tms_], inc="sc")
                pl.flush()
            es1.close()

            es2 = contextlib.ExitStack()
            al2 = lambda n, sh, dt: es2.enter_context(nc.sbuf_tensor(n + "_s%d" % s, sh, dt))
            qst0 = al2("qst0", [128, T], BF16)
            bstat = al2("bstat", [128, 13312], BF16)
            KOFF, ZCOFF, Z2OFF, WOFF = 0, 2048, 3072, 5120
            kst0 = bstat[:, KOFF:KOFF + T]
            pbuf0 = al2("pbuf0", [128, 12 * QT], BF16)
            pbuf1 = al2("pbuf1", [128, NTC * QT], BF16)
            opk = al2("opk", [128, NCH * T], BF16)
            zinv0 = al2("zinv0", [32, QT], BF16)
            zinv1 = al2("zinv1", [32, QT], BF16)
            x1c0 = al2("x1c0", [128, QT], F32)
            x2st0 = al2("x2st0", [128, QT], F32)

            if True:
                pl.barrier()
                pl.dma("sync", bstat[:, ZCOFF:ZCOFF + 1024], cb_ZC.ap())
                zc_tok = pl.dma("sync", bstat[0:32, Z2OFF:Z2OFF + 2048], cb_Z2C.ap())
                wotok = []
                for g in range(4):
                    wotok.append(pl.dma(
                        "gpsimd", bstat[:, WOFF + g * 2 * D:WOFF + (g + 1) * 2 * D]
                        .rearrange("p (c m) -> p c m", c=2),
                        w_out.ap()[g * 256:(g + 1) * 256, :]
                        .rearrange("(c p) m -> p c m", p=128)))
                qst = [qst0]
                kst = [kst0]
                pl.op("gpsimd", "memset", qst0[64:128, :], 0.0, inc="sg")
                zpad = pl.op("gpsimd", "memset", kst0[64:128, :], -1.0, inc="sg")
                pbufs = [pbuf0, pbuf1]
                qst_free = [None]
                kst_free = [None]
                pbuf_free = [None, None]
                obank_free = [None, None]
                zfirst = [True, True]
                ptok = {}
                oev = {}
                qst_built = [None]
                kst_built = [None]
                rot = [0]

                def rot_next():
                    rot[0] = (rot[0] + 1) % 3
                    return rot[0]

                def build_staging(h):
                    mch = h // 2
                    psel = (h % 2) * 128
                    evq = evk = None
                    for blk in range(4):
                        for which in range(2):
                            src = qsb if which == 0 else ksb
                            dstt = qst[0] if which == 0 else kst[0]
                            fr = qst_free[0] if which == 0 else kst_free[0]
                            bk = rot_next()
                            pl.op("tensor", "matmul", pb[bk][0:128, 0:512],
                                  iselp[:, psel:psel + 128],
                                  src[:, mch * T + blk * 512:mch * T + (blk + 1) * 512],
                                  start=True, stop=True,
                                  waits=[consts_tok, fr, bank_free[bk]], inc="st")
                            ev = pl.op("vector", "tensor_copy",
                                       dstt[0:64, blk * 512:(blk + 1) * 512],
                                       pb[bk][0:64, 0:512],
                                       waits=[pl.tok("st")], inc="sv")
                            bank_free[bk] = ev
                            if which == 0:
                                evq = ev
                            else:
                                evk = ev
                    tdm = pl.dma("gpsimd", qst[0][64:65, :], mall[h:h + 1, :],
                                 waits=[mall_tok, qst_free[0], zpad])
                    qst_built[0] = (evq, tdm)
                    kst_built[0] = (evk, zpad)

                def gen_scores(h, qb):
                    pp = qb % 2
                    nkc = 4 * qb + 4
                    mz = [None]
                    for j in range(1, 4):
                        kc = 4 * qb + j
                        mz.append(pl.op("gpsimd", "memset",
                                        pbufs[pp][:, kc * QT:kc * QT + 128 * j],
                                        0.0, waits=[pbuf_free[pp]], inc="sg"))
                    for kc in range(nkc):
                        bk = rot_next()
                        w = list(qst_built[0]) + list(kst_built[0]) + [bank_free[bk]]
                        pl.op("tensor", "matmul", pb[bk][:, 0:QT],
                              kst[0][:, kc * 128:(kc + 1) * 128],
                              qst[0][:, qb * QT:(qb + 1) * QT],
                              start=True, stop=True, waits=w, inc="st")
                        tS = pl.tok("st")
                        j = kc - 4 * qb
                        if j >= 0:
                            texp = pl.op("scalar", "activation",
                                         pbufs[pp][:, kc * QT + 128 * j:(kc + 1) * QT],
                                         pb[bk][:, 128 * j:QT], AF.Exp,
                                         waits=[tS, pbuf_free[pp]], inc="sc")
                            ttri = pl.op("vector", "tensor_tensor",
                                         pbufs[pp][:, kc * QT + 128 * j:
                                                   kc * QT + 128 * (j + 1)],
                                         pbufs[pp][:, kc * QT + 128 * j:
                                                   kc * QT + 128 * (j + 1)],
                                         tri01[:, :], ALU.mult,
                                         waits=[texp, consts_tok], inc="sv")
                            ptok[(pp, kc)] = [texp, ttri, mz[j] if j >= 1 else None]
                        else:
                            texp = pl.op("scalar", "activation",
                                         pbufs[pp][:, kc * QT:(kc + 1) * QT],
                                         pb[bk][:, 0:QT], AF.Exp,
                                         waits=[tS, pbuf_free[pp]], inc="sc")
                            ptok[(pp, kc)] = [texp]
                        bank_free[bk] = texp
                        yield

                def gen_av(h, qb):
                    pp = qb % 2
                    nkc = 4 * qb + 4
                    odd = h % 2 == 1
                    ob = 3 + qb % 2
                    rows = slice(64, 128) if odd else slice(0, 64)
                    for kc in range(nkc):
                        w = list(ptok[(pp, kc)]) + (
                            [obank_free[qb % 2]] if kc == 0 else [])
                        vap = vsb[:, kc * 1024 + (h // 2) * 128:
                                  kc * 1024 + (h // 2) * 128 + 128]
                        pl.op("tensor", "matmul", pb[ob][0:128, 0:QT],
                              vap, pbufs[pp][:, kc * QT:(kc + 1) * QT],
                              start=(kc == 0), stop=(kc == nkc - 1),
                              waits=w, inc=("st" if kc == nkc - 1 else None))
                        yield
                    tav = pl.tok("st")
                    osl = opk[rows, (h // 2) * T + qb * QT:
                              (h // 2) * T + (qb + 1) * QT]
                    tOe = pl.op("vector", "tensor_copy", osl,
                                pb[ob][rows, 0:QT], waits=[tav], inc="sv")
                    obank_free[qb % 2] = tOe
                    oev[(h, qb)] = tOe

                def gen_z(h, qb):
                    pp = qb % 2
                    nkc = 4 * qb + 4
                    zbk = 5 + (qb % 2)
                    j = 16 * (qb >> 1) + h
                    for kc in range(nkc):
                        wz = [zc_tok] + list(ptok[(pp, kc)]) + (
                            [bank_free[zbk]] if zfirst[qb % 2] else [])
                        pl.op("tensor", "matmul", pb[zbk][0:32, 0:QT],
                              bstat[:, ZCOFF + j * 32:ZCOFF + (j + 1) * 32],
                              pbufs[pp][:, kc * QT:(kc + 1) * QT],
                              start=zfirst[qb % 2],
                              stop=(h == H - 1 and qb >= 2 and kc == nkc - 1),
                              waits=wz, inc=("st" if kc == nkc - 1 else None))
                        zfirst[qb % 2] = False
                        yield
                    pbuf_free[pp] = pl.tok("st")

                def drive(gens, run=3):
                    gens = [g for g in gens if g is not None]
                    while gens:
                        nxt = []
                        for g in gens:
                            alive = True
                            for _ in range(run):
                                try:
                                    next(g)
                                except StopIteration:
                                    alive = False
                                    break
                            if alive:
                                nxt.append(g)
                        gens = nxt

                build_staging(0)
                prev = None
                for h in range(H):
                    for qb in range(NQB):
                        gs = gen_scores(h, qb)
                        if prev is not None:
                            drive([gs, gen_av(*prev), gen_z(*prev)])
                        else:
                            drive([gs])
                        if qb == NQB - 1:
                            qst_free[0] = pl.tok("st")
                            kst_free[0] = pl.tok("st")
                            if h + 1 < H:
                                build_staging(h + 1)
                        prev = (h, qb)
                drive([gen_av(*prev), gen_z(*prev)])

                t_zstop = pl.tok("st")
                zinvs = [zinv0, zinv1]
                trec = [None, None]
                for par in range(2):
                    trec[par] = pl.op("vector", "reciprocal", zinvs[par][:, :],
                                      pb[5 + par][0:32, 0:QT],
                                      waits=[t_zstop], inc="sv")
                    bank_free[5 + par] = trec[par]
                tnorm = {}
                for qb in range(NQB):
                    for kc in range(NCH):
                        bk = (qb * NCH + kc) % 2
                        slot = 8 * (qb >> 1) + kc
                        pl.op("tensor", "matmul", pb[bk][:, 0:QT],
                              bstat[0:32, Z2OFF + slot * 128:Z2OFF + (slot + 1) * 128],
                              zinvs[qb % 2][:, :], start=True, stop=True,
                              waits=[trec[qb % 2], bank_free[bk], zc_tok],
                              inc="st")
                        osl = opk[:, kc * T + qb * QT:kc * T + (qb + 1) * QT]
                        tn = pl.op("vector", "tensor_tensor", osl, osl,
                                   pb[bk][:, 0:QT], ALU.mult,
                                   waits=[pl.tok("st"), oev[(2 * kc, qb)],
                                          oev[(2 * kc + 1, qb)]], inc="sv")
                        bank_free[bk] = tn
                        tnorm[(qb, kc)] = tn

                x1c = [x1c0]
                x2st = [x2st0]
                x1c_free = [None]
                x2st_free = [None]
                for qb in range(NQB):
                    for oc in range(NCH):
                        r = 0
                        tld = pl.dma("sync", x1c[r][:, :],
                                     x1s.ap()[s, oc, :, qb * QT:(qb + 1) * QT],
                                     waits=[x1c_free[r]])
                        bk = 7
                        for kc in range(NCH):
                            w = [wotok[kc // 2], tnorm[(qb, kc)]] + (
                                [bank_free[bk]] if kc == 0 else [])
                            pl.op("tensor", "matmul", pb[bk][:, 0:QT],
                                  bstat[:, WOFF + kc * D + oc * 128:
                                        WOFF + kc * D + (oc + 1) * 128],
                                  opk[:, kc * T + qb * QT:kc * T + (qb + 1) * QT],
                                  start=(kc == 0), stop=(kc == NCH - 1),
                                  waits=w, inc=("st" if kc == NCH - 1 else None))
                        tmm = pl.tok("st")
                        tev = pl.op("vector", "scalar_tensor_tensor", x2st[r][:, :],
                                    pb[bk][:, 0:QT], bout_sb[:, oc:oc + 1],
                                    x1c[r][:, :], op0=ALU.add, op1=ALU.add,
                                    waits=[tmm, tld, x2st_free[r], consts_tok],
                                    inc="sv")
                        bank_free[bk] = tev
                        x1c_free[r] = tev
                        x2st_free[r] = pl.dma(
                            "gpsimd", x2s.ap()[s, oc, :, qb * QT:(qb + 1) * QT],
                            x2st[r][:, :], waits=[tev])
                pl.flush()
            es2.close()
        es0.close()
        for i in range(8):
            bank_free[i] = None

    ffn_phase(first=True)
    for s in range(S):
        attn_phase(s)
    ffn_phase(first=False)
    pl.barrier()
    pl.flush()
    stack.close()
    return nc


_NC_CACHE = {}


def _consts():
    ident = np.eye(128, dtype=np.float32)
    ones65 = np.zeros((65, 128), dtype=np.float32)
    ones65[0, :] = 1.0
    ones65[64, :] = 1.0
    isel = np.eye(128).astype(ml_dtypes.bfloat16)
    tri01 = np.triu(np.ones((128, 128))).astype(ml_dtypes.bfloat16)
    onescol = np.zeros((128, 128), dtype=ml_dtypes.bfloat16)
    onescol[:, 0] = 1.0
    E = np.zeros((128, 8 * 128), dtype=np.float32)
    for mch in range(8):
        for p in range(128):
            E[p, mch * 128 + 2 * mch + (p // 64)] = 1.0
    iselp = np.zeros((128, 256), dtype=np.float32)
    iselp[0:64, 0:64] = np.eye(64)
    iselp[64:128, 128:192] = np.eye(64)
    ZCm = np.zeros((128, 32 * 32), dtype=np.float32)
    for j in range(32):
        ZCm[:, j * 32 + j] = 1.0
    Z2Cm = np.zeros((32, 16 * 128), dtype=np.float32)
    for slot in range(16):
        for i in range(128):
            Z2Cm[2 * slot + i // 64, slot * 128 + i] = 1.0
    return (ident, ones65, isel, tri01, onescol, E.astype(ml_dtypes.bfloat16),
            ZCm.astype(ml_dtypes.bfloat16), Z2Cm.astype(ml_dtypes.bfloat16),
            iselp.astype(ml_dtypes.bfloat16))


def kernel(L0_attn_x, L0_attn_w_qkv, L0_attn_w_out, L0_attn_b_out,
           L0_ffn_W1, L0_ffn_b1, L0_ffn_W2, L0_ffn_b2):
    x = np.ascontiguousarray(np.asarray(L0_attn_x, dtype=np.float32))
    ident, ones65, isel, tri01, onescol, E, ZCm, Z2Cm, iselp_c = _consts()
    common = {
        "w_qkv": np.ascontiguousarray(np.asarray(L0_attn_w_qkv, dtype=np.float32)),
        "w_out": np.ascontiguousarray(np.asarray(L0_attn_w_out, dtype=np.float32)),
        "b_out": np.ascontiguousarray(np.asarray(L0_attn_b_out, dtype=np.float32)),
        "w1": np.ascontiguousarray(np.asarray(L0_ffn_W1, dtype=np.float32)),
        "b1": np.ascontiguousarray(np.asarray(L0_ffn_b1, dtype=np.float32)),
        "w2": np.ascontiguousarray(np.asarray(L0_ffn_W2, dtype=np.float32)),
        "b2": np.ascontiguousarray(np.asarray(L0_ffn_b2, dtype=np.float32)),
        "c_ident": ident, "c_ones65": ones65, "cb_isel": isel,
        "cb_tri01": tri01, "cb_onescol": onescol, "cb_E": E,
        "cb_ZC": ZCm, "cb_Z2C": Z2Cm, "cb_iselp": iselp_c,
    }
    if "nc" not in _NC_CACHE:
        _NC_CACHE["nc"] = build_program()
    nc = _NC_CACHE["nc"]
    in_maps = []
    for c in range(NCORES):
        m = dict(common)
        m["x"] = np.ascontiguousarray(x[c * S:(c + 1) * S])
        in_maps.append(m)
    res = run_bass_kernel_spmd(nc, in_maps, core_ids=list(range(NCORES)))
    return np.concatenate([np.asarray(r["out"]) for r in res.results], axis=0)


# version 08
# speedup vs baseline: 1.0635x; 1.0007x over previous
import contextlib

import numpy as np
import ml_dtypes
import concourse.bass as bass
import concourse.mybir as mybir
from concourse.bass_utils import run_bass_kernel_spmd

F32 = mybir.dt.float32
BF16 = mybir.dt.bfloat16
AF = mybir.ActivationFunctionType
ALU = mybir.AluOpType
AX = mybir.AxisListType

NCORES = 8
S = 2
T = 2048
D = 1024
H = 16
INTER = 4096
NCH = D // 128
NIC = INTER // 128
FT = 256
NFT = T // FT
QT = 512
NQB = T // QT
NTC = T // 128
VW = H * 65

ENGS = ("sync", "tensor", "vector", "scalar", "gpsimd")


class Plan:

    def __init__(self, nc, sems):
        self.nc = nc
        self.sems = sems
        self.q = {e: [] for e in ENGS}
        self.cnt = {name: 0 for name in sems}
        self.lastwait = {e: {} for e in ENGS}

    def op(self, eng, meth, *args, waits=(), inc=None, inc_amt=1, **kw):
        self.wait_only(eng, waits)
        tok = None
        if inc is not None:
            self.cnt[inc] += inc_amt
            tok = (inc, self.cnt[inc])
        self.q[eng].append(("c", meth, args, kw, inc, inc_amt))
        return tok

    def dma(self, eng, out, in_, waits=()):
        sname = "sd_s" if eng == "sync" else "sd_g"
        return self.op(eng, "dma_start", out, in_, waits=waits, inc=sname, inc_amt=16)

    def wait_only(self, eng, waits):
        lw = self.lastwait[eng]
        for tok in waits:
            if tok is None:
                continue
            sname, val = tok
            if val <= 0 or lw.get(sname, -1) >= val:
                continue
            assert val <= self.cnt[sname], (eng, sname, val, self.cnt[sname])
            lw[sname] = val
            self.q[eng].append(("w", sname, val))

    def tok(self, sname):
        return (sname, self.cnt[sname])

    def barrier(self):
        toks = [(n, c) for n, c in self.cnt.items() if c > 0]
        for e in ENGS:
            self.wait_only(e, toks)

    def flush(self):
        qs = self.q
        self.q = {e: [] for e in ENGS}
        if not any(qs[e] for e in ENGS):
            return
        with self.nc.allow_low_precision(
                reason="bf16 operands / fp32 accumulation per problem tolerance"), \
                self.nc.Block() as blk:
            for e in ENGS:
                if not qs[e]:
                    continue

                def runner(engobj, items=qs[e]):
                    for it in items:
                        if it[0] == "w":
                            engobj.wait_ge(self.sems[it[1]], it[2])
                        else:
                            _, meth, args, kw, inc, amt = it
                            r = getattr(engobj, meth)(*args, **kw)
                            if inc is not None:
                                r.then_inc(self.sems[inc], amt)

                getattr(blk, e)(runner)


def build_program():
    nc = bass.Bass("TRN2", target_bir_lowering=False)

    x_in = nc.dram_tensor("x", [S, T, D], F32, kind="ExternalInput")
    w_qkv = nc.dram_tensor("w_qkv", [D, 3 * D], F32, kind="ExternalInput")
    w_out = nc.dram_tensor("w_out", [D, D], F32, kind="ExternalInput")
    b_out = nc.dram_tensor("b_out", [D], F32, kind="ExternalInput")
    w1 = nc.dram_tensor("w1", [D, INTER], F32, kind="ExternalInput")
    b1 = nc.dram_tensor("b1", [INTER], F32, kind="ExternalInput")
    w2 = nc.dram_tensor("w2", [INTER, D], F32, kind="ExternalInput")
    b2 = nc.dram_tensor("b2", [D], F32, kind="ExternalInput")
    c_ident = nc.dram_tensor("c_ident", [128, 128], F32, kind="ExternalInput")
    c_ones65 = nc.dram_tensor("c_ones65", [65, 128], F32, kind="ExternalInput")
    cb_isel = nc.dram_tensor("cb_isel", [128, 128], BF16, kind="ExternalInput")
    cb_tri01 = nc.dram_tensor("cb_tri01", [128, 128], BF16, kind="ExternalInput")
    cb_onescol = nc.dram_tensor("cb_onescol", [128, 128], BF16, kind="ExternalInput")
    cb_E = nc.dram_tensor("cb_E", [128, 8 * 128], BF16, kind="ExternalInput")
    cb_iselp = nc.dram_tensor("cb_iselp", [128, 256], BF16, kind="ExternalInput")
    cb_ZC = nc.dram_tensor("cb_ZC", [128, 32 * 32], BF16, kind="ExternalInput")
    cb_Z2C = nc.dram_tensor("cb_Z2C", [32, 16 * 128], BF16, kind="ExternalInput")
    out = nc.dram_tensor("out", [S, T, D], F32, kind="ExternalOutput")
    x1s = nc.dram_tensor("x1s", [S, NCH, 128, T], F32)
    x2s = nc.dram_tensor("x2s", [S, NCH, 128, T], F32)

    stack = contextlib.ExitStack()
    sems = {}
    for name in ("sd_s", "sd_g", "st", "sv", "sc", "sg"):
        sems[name] = stack.enter_context(nc.semaphore(name))
    pl = Plan(nc, sems)

    sb = lambda name, shape, dt: stack.enter_context(nc.sbuf_tensor(name, shape, dt))
    ident = sb("ident", [128, 128], F32)
    ones65 = sb("ones65", [65, 128], F32)
    isel = sb("isel", [128, 128], BF16)
    tri01 = sb("tri01", [128, 128], BF16)
    onescol = sb("onescol", [128, 128], BF16)
    iselp = sb("iselp", [128, 256], BF16)
    b1_sb = sb("b1sb", [128, NIC], F32)
    b2_sb = sb("b2sb", [128, NCH], F32)
    bout_sb = sb("boutsb", [128, NCH], F32)
    pb = [stack.enter_context(nc.psum_tensor("pb%d" % i, [128, 512], F32))
          for i in range(8)]
    bank_free = [None] * 8

    for dst, src in ((ident, c_ident), (ones65, c_ones65), (isel, cb_isel),
                     (tri01, cb_tri01), (onescol, cb_onescol), (iselp, cb_iselp)):
        pl.dma("sync", dst[:, :], src.ap())
    pl.op("sync", "dma_start", b1_sb[:, :],
          b1.ap().rearrange("(i p) -> p i", p=128),
          allow_slow_non_contiguous=True, inc="sd_s", inc_amt=16)
    pl.op("sync", "dma_start", b2_sb[:, :],
          b2.ap().rearrange("(i p) -> p i", p=128),
          allow_slow_non_contiguous=True, inc="sd_s", inc_amt=16)
    consts_tok = pl.op("sync", "dma_start", bout_sb[:, :],
                       b_out.ap().rearrange("(i p) -> p i", p=128),
                       allow_slow_non_contiguous=True, inc="sd_s", inc_amt=16)

    def emit_ln(src, xbf, sq, tmp, lnt, ntok, src_ready, bk_sx, bk_sq, bk_bc,
                lnrow, bcst, in_place=False, xbf_free=None, lnt_free=None,
                cstride=None, col_off=0):
        if cstride is None:
            cstride = ntok
        csl = lambda buf: (buf[:, 0:NCH * cstride]
                           .rearrange("p (c f) -> p c f", c=NCH)[
                               :, :, col_off:col_off + ntok])
        ck = lambda buf, ch: buf[:, ch * cstride + col_off:
                                 ch * cstride + col_off + ntok]
        t_xbf = pl.op("vector", "tensor_copy", csl(xbf), csl(src),
                      waits=[src_ready, consts_tok, xbf_free], inc="sv")
        if sq is not None:
            t_sq = pl.op("vector", "tensor_tensor", csl(sq), csl(xbf),
                         csl(xbf), ALU.mult, inc="sv")
            for ch in range(NCH):
                for bk, moving, tw in ((bk_sx, xbf, t_xbf), (bk_sq, sq, t_sq)):
                    w = [tw] + ([bank_free[bk]] if ch == 0 else [])
                    pl.op("tensor", "matmul", pb[bk][0:128, 0:ntok],
                          onescol[:, :], ck(moving, ch),
                          start=(ch == 0), stop=(ch == NCH - 1),
                          waits=w, inc=("st" if ch == NCH - 1 else None))
        else:
            for ch in range(NCH):
                w = [t_xbf] + ([bank_free[bk_sx]] if ch == 0 else [])
                pl.op("tensor", "matmul", pb[bk_sx][0:128, 0:ntok],
                      onescol[:, :], ck(xbf, ch),
                      start=(ch == 0), stop=(ch == NCH - 1),
                      waits=w, inc=("st" if ch == NCH - 1 else None))
            t_sx = pl.tok("st")
            t_sq = pl.op("vector", "tensor_tensor", csl(xbf), csl(xbf),
                         csl(xbf), ALU.mult, waits=[t_sx], inc="sv")
            for ch in range(NCH):
                w = [t_sq] + ([bank_free[bk_sq]] if ch == 0 else [])
                pl.op("tensor", "matmul", pb[bk_sq][0:128, 0:ntok],
                      onescol[:, :], ck(xbf, ch),
                      start=(ch == 0), stop=(ch == NCH - 1),
                      waits=w, inc=("st" if ch == NCH - 1 else None))
        t_stats = pl.op("tensor", "engine_nop", inc="st") if False else pl.tok("st")
        mean = lnrow[0:1, 0:ntok]
        ex2 = lnrow[0:1, ntok:2 * ntok]
        ve = lnrow[0:1, 2 * ntok:3 * ntok]
        r0 = lnrow[0:1, 3 * ntok:4 * ntok]
        e1 = ex2
        pl.op("vector", "tensor_scalar", mean, pb[bk_sx][0:1, 0:ntok],
              1.0 / D, None, op0=ALU.mult, waits=[t_stats], inc="sv")
        pl.op("vector", "tensor_scalar", ex2, pb[bk_sq][0:1, 0:ntok],
              1.0 / D, None, op0=ALU.mult, inc="sv")
        pl.op("vector", "tensor_tensor", ve, mean, mean, ALU.mult, inc="sv")
        t_ve = pl.op("vector", "scalar_tensor_tensor", ve, ex2, 1e-5, ve,
                     op0=ALU.add, op1=ALU.subtract, inc="sv")
        bank_free[bk_sx] = t_ve
        bank_free[bk_sq] = t_ve
        t_sq2 = pl.op("scalar", "activation", r0, ve, AF.Sqrt, waits=[t_ve], inc="sc")
        pl.op("vector", "reciprocal", r0, r0, waits=[t_sq2], inc="sv")
        pl.op("vector", "tensor_tensor", e1, r0, r0, ALU.mult, inc="sv")
        pl.op("vector", "tensor_tensor", e1, e1, ve, ALU.mult, inc="sv")
        pl.op("vector", "tensor_scalar", e1, e1, -0.5, 1.5, op0=ALU.mult,
              op1=ALU.add, inc="sv")
        pl.op("vector", "tensor_tensor", bcst[0:1, 0:ntok], r0, e1, ALU.mult,
              inc="sv")
        t_row = pl.op("vector", "tensor_tensor", bcst[0:1, ntok:2 * ntok], mean,
                      bcst[0:1, 0:ntok], ALU.mult, inc="sv")
        if 2 * ntok <= 512:
            t_bc = pl.op("tensor", "matmul", pb[bk_bc][:, 0:2 * ntok],
                         ones65[0:1, :], bcst[0:1, 0:2 * ntok], start=True,
                         stop=True, waits=[t_row, bank_free[bk_bc]], inc="st")
            Rb = pb[bk_bc][:, 0:ntok]
            MRb = pb[bk_bc][:, ntok:2 * ntok]
        else:
            pl.op("tensor", "matmul", pb[bk_bc][:, 0:ntok],
                  ones65[0:1, :], bcst[0:1, 0:ntok], start=True, stop=True,
                  waits=[t_row, bank_free[bk_bc]], inc="st")
            t_bc = pl.op("tensor", "matmul", pb[bk_sx][:, 0:ntok],
                         ones65[0:1, :], bcst[0:1, ntok:2 * ntok], start=True,
                         stop=True, waits=[t_row, bank_free[bk_sx]], inc="st")
            Rb = pb[bk_bc][:, 0:ntok]
            MRb = pb[bk_sx][:, 0:ntok]
        last = None
        for ch in range(NCH):
            dst = src if in_place else tmp
            pl.op("vector", "tensor_tensor", ck(dst, ch), ck(src, ch), Rb,
                  ALU.mult, waits=[t_bc, lnt_free], inc="sv")
            last = pl.op("vector", "tensor_tensor", ck(lnt, ch), ck(dst, ch),
                         MRb, ALU.subtract, inc="sv")
        bank_free[bk_bc] = last
        if 2 * ntok > 512:
            bank_free[bk_sx] = last
        return last

    def ffn_phase(first):
        es = contextlib.ExitStack()
        sfx = "_p1" if first else "_p3"
        al = lambda n, sh, dt: es.enter_context(nc.sbuf_tensor(n + sfx, sh, dt))
        w1sb = al("w1sb", [128, NCH * INTER], BF16)
        w2sb = al("w2sb", [128, NIC * D], BF16)
        hbuf = al("hbuf", [128, NIC * FT], BF16)
        xfm0 = al("xfm0", [128, NCH * FT], F32)
        xfm1 = al("xfm1", [128, NCH * FT], F32)
        if first:
            xq0 = al("xq0", [128, D], F32)
            xq1 = al("xq1", [128, D], F32)
        else:
            xq0 = xq1 = None
        lnt0 = al("lnt0", [128, NCH * FT], BF16)
        lnt1 = al("lnt1", [128, NCH * FT], BF16)
        xbf = al("xbf_f", [128, NCH * FT], BF16)
        sqb = al("sq_f", [128, NCH * FT], BF16)
        tmp = al("tmp_f", [128, NCH * FT], F32)
        ytmp = al("ytmp", [128, FT], F32)
        lnrow = al("lnrow_f", [1, 4 * FT], F32)
        bcst = al("bcst_f", [1, 2 * FT], F32)
        if not first:
            tm0 = al("tm0", [128, D], F32)
            tm1 = al("tm1", [128, D], F32)
        else:
            tm0 = tm1 = None
        if True:
            pl.barrier()
            xfm = [xfm0, xfm1]
            xq = [xq0, xq1]
            lnts = [lnt0, lnt1]
            tms = [tm0, tm1]
            w1tok = []
            for kc in range(NCH):
                w1tok.append(pl.dma("gpsimd", w1sb[:, kc * INTER:(kc + 1) * INTER],
                                    w1.ap()[kc * 128:(kc + 1) * 128, :]))
            w2tok = []
            w2v = w2.ap().rearrange("(i p) m -> p i m", p=128)
            for g in range(4):
                w2tok.append(pl.dma(
                    "gpsimd",
                    w2sb[:, g * 8 * D:(g + 1) * 8 * D].rearrange("p (i m) -> p i m", i=8),
                    w2v[:, g * 8:(g + 1) * 8, :]))

            xq_free = [None, None]
            xfm_free = [None, None]
            lnt_ready = [None, None]
            lnt_free = [None, None]
            xbf_free = None
            hready = [None] * NIC
            tm_free = [None, None]

            def emit_lnchain(s, t):
                nonlocal xbf_free
                cur = xfm[t % 2]
                if first:
                    src_ready = None
                    for qi in range(2):
                        buf = xq[qi]
                        ld = pl.dma("sync", buf[:, :],
                                    x_in.ap()[s, t * FT + qi * 128:
                                              t * FT + (qi + 1) * 128, :],
                                    waits=[xq_free[qi]])
                        for j in range(NCH):
                            bk = 6 + (j // 4)
                            w = [ld, consts_tok] + ([bank_free[bk]] if j % 4 == 0 else [])
                            pl.op("tensor", "transpose",
                                  pb[bk][:, (j % 4) * 128:(j % 4 + 1) * 128],
                                  buf[:, j * 128:(j + 1) * 128], ident[:, :],
                                  waits=w, inc=("st" if j % 4 == 3 else None))
                            if j % 4 == 3:
                                ttr = pl.tok("st")
                                dst = cur[:, :].rearrange("p (c f) -> p c f", c=NCH)[
                                    :, j - 3:j + 1, qi * 128:(qi + 1) * 128]
                                ev = pl.op("vector", "tensor_copy", dst,
                                           pb[bk][:, 0:512].rearrange(
                                               "p (c f) -> p c f", c=4),
                                           waits=[ttr, xfm_free[t % 2]], inc="sv")
                                bank_free[bk] = ev
                                src_ready = ev
                        xq_free[qi] = pl.tok("st")
                else:
                    src_ready = pl.dma(
                        "sync", cur[:, :].rearrange("p (c f) -> p c f", c=NCH),
                        x2s.ap()[s, :, :, t * FT:(t + 1) * FT].transpose([1, 0, 2]),
                        waits=[xfm_free[t % 2]])
                lnt_ready[t % 2] = emit_ln(
                    cur, xbf, sqb, tmp, lnts[t % 2], FT, src_ready, 4, 5, 5,
                    lnrow, bcst, xbf_free=xbf_free, lnt_free=lnt_free[t % 2])
                xbf_free = pl.tok("st")

            def emit_ffn(s, t):
                cur = xfm[t % 2]
                lnt = lnts[t % 2]
                for ic in range(NIC):
                    bk = ic % 2
                    for kc in range(NCH):
                        w = [lnt_ready[t % 2], w1tok[kc]] + (
                            [bank_free[bk]] if kc == 0 else [])
                        pl.op("tensor", "matmul", pb[bk][:, 0:FT],
                              w1sb[:, kc * INTER + ic * 128:kc * INTER + (ic + 1) * 128],
                              lnt[:, kc * FT:(kc + 1) * FT],
                              start=(kc == 0), stop=(kc == NCH - 1),
                              waits=w, inc=("st" if kc == NCH - 1 else None))
                    tmm = pl.tok("st")
                    hready[ic] = pl.op("scalar", "activation",
                                       hbuf[:, ic * FT:(ic + 1) * FT],
                                       pb[bk][:, 0:FT], AF.Relu,
                                       bias=b1_sb[:, ic:ic + 1],
                                       waits=[tmm], inc="sc")
                    bank_free[bk] = hready[ic]
                lnt_free[t % 2] = pl.tok("st")
                ev = None
                for oc in range(NCH):
                    bk = 2 + oc % 2
                    for ic in range(NIC):
                        w = [hready[ic], w2tok[ic // 8]] + (
                            [bank_free[bk]] if ic == 0 else [])
                        pl.op("tensor", "matmul", pb[bk][:, 0:FT],
                              w2sb[:, ic * D + oc * 128:ic * D + (oc + 1) * 128],
                              hbuf[:, ic * FT:(ic + 1) * FT],
                              start=(ic == 0), stop=(ic == NIC - 1),
                              waits=w, inc=("st" if ic == NIC - 1 else None))
                    tmm = pl.tok("st")
                    pl.op("vector", "tensor_scalar", ytmp[:, :], pb[bk][:, 0:FT],
                          b2_sb[:, oc:oc + 1], 0.5, op0=ALU.add, op1=ALU.mult,
                          waits=[tmm], inc="sv")
                    ev = pl.op("vector", "tensor_tensor",
                               cur[:, oc * FT:(oc + 1) * FT], ytmp[:, :],
                               cur[:, oc * FT:(oc + 1) * FT], ALU.add, inc="sv")
                    bank_free[bk] = ev
                if first:
                    xfm_free[t % 2] = pl.dma(
                        "gpsimd",
                        x1s.ap()[s, :, :, t * FT:(t + 1) * FT].transpose([1, 0, 2]),
                        cur[:, :].rearrange("p (c f) -> p c f", c=NCH), waits=[ev])
                else:
                    for hf in range(2):
                        tmb = tms[hf]
                        for j in range(NCH):
                            bk = 6 + (j // 4)
                            w = [ev] + ([bank_free[bk]] if j % 4 == 0 else [])
                            pl.op("tensor", "transpose",
                                  pb[bk][:, (j % 4) * 128:(j % 4 + 1) * 128],
                                  cur[:, j * FT + hf * 128:j * FT + hf * 128 + 128],
                                  ident[:, :],
                                  waits=w, inc=("st" if j % 4 == 3 else None))
                            if j % 4 == 3:
                                ttr = pl.tok("st")
                                evt = pl.op("vector", "tensor_copy",
                                            tmb[:, (j - 3) * 128:(j + 1) * 128],
                                            pb[bk][:, 0:512],
                                            waits=[ttr, tm_free[hf]], inc="sv")
                                bank_free[bk] = evt
                        tm_free[hf] = pl.dma(
                            "gpsimd",
                            out.ap()[s, t * FT + hf * 128:t * FT + hf * 128 + 128, :],
                            tmb[:, :], waits=[pl.tok("sv")])
                    xfm_free[t % 2] = pl.tok("st")

            for s in range(S):
                for t in range(NFT + 1):
                    if t < NFT:
                        emit_lnchain(s, t)
                    if t > 0:
                        emit_ffn(s, t - 1)
            pl.flush()
        es.close()
        for i in range(8):
            bank_free[i] = None

    def attn_phase(s):
        es0 = contextlib.ExitStack()
        al0 = lambda n, sh, dt: es0.enter_context(nc.sbuf_tensor(n + "_s%d" % s, sh, dt))
        qsb = al0("qsb", [128, NCH * T], BF16)
        ksb = al0("ksb", [128, NCH * T], BF16)
        astat = al0("astat", [128, 21504], BF16)
        KOFF, ZCOFF, Z2OFF, VOFF = 0, 2048, 3072, 5120
        vsb = astat[:, VOFF:VOFF + NTC * 1024]
        mall = al0("mall", [16, T], BF16)
        if True:
            vsb4 = vsb[:, :].rearrange("p (t h d) -> p t h d", t=NTC, h=H)
            mall_tok = None
            es1 = contextlib.ExitStack()
            al1 = lambda n, sh, dt: es1.enter_context(nc.sbuf_tensor(n + "_s%d" % s, sh, dt))
            wq = al1("wqkvsb", [128, NCH * 3 * D], BF16)
            x1a = al1("x1a", [128, NCH * QT], F32)
            lnta0 = al1("lnta0", [128, NCH * QT], BF16)
            lnta1 = al1("lnta1", [128, NCH * QT], BF16)
            sqh0 = al1("sqh0", [128, QT], BF16)
            sqh1 = sqh0
            nk = al1("nk", [16, T], BF16)
            nq = mall
            kmax = al1("kmax", [16, 1], F32)
            lnrow = al1("lnrow_a", [1, 4 * 256], F32)
            bcst = al1("bcst_a", [1, 512], F32)
            Esel = al1("Esel", [128, 8 * 128], BF16)
            if True:
                pl.barrier()
                lntas = [lnta0, lnta1]
                esel_tok = pl.dma("sync", Esel[:, :], cb_E.ap())
                wqtok = []
                for kc in range(NCH):
                    wqtok.append(pl.dma("gpsimd", wq[:, kc * 3 * D:(kc + 1) * 3 * D],
                                        w_qkv.ap()[kc * 128:(kc + 1) * 128, :]))
                x1a_free = None
                lnta_ready = [None, None]
                lnta_free = [None, None]
                sqh_free = [None, None]
                sqhs = [sqh0, sqh1]

                def emit_lnchain_a(qb):
                    nonlocal x1a_free
                    src_ready = pl.dma(
                        "sync", x1a[:, :].rearrange("p (c f) -> p c f", c=NCH),
                        x1s.ap()[s, :, :, qb * QT:(qb + 1) * QT].transpose([1, 0, 2]),
                        waits=[x1a_free])
                    for half in range(2):
                        lnta_ready[qb % 2] = emit_ln(
                            x1a, lntas[qb % 2], None, None, lntas[qb % 2], 256,
                            src_ready, 4, 5, 5, lnrow, bcst, in_place=True,
                            xbf_free=lnta_free[qb % 2],
                            lnt_free=lnta_free[qb % 2],
                            cstride=QT, col_off=half * 256)
                    x1a_free = lnta_ready[qb % 2]

                def emit_qkv(qb):
                    nonlocal sqh_free
                    lnta = lntas[qb % 2]
                    for which, dstsb, nbank, scale in ((0, qsb, 6, 0.125),
                                                       (1, ksb, 7, 1.0)):
                        for mch in range(NCH):
                            bk = mch % 2
                            for kc in range(NCH):
                                w = [lnta_ready[qb % 2], wqtok[kc]] + (
                                    [bank_free[bk]] if kc == 0 else [])
                                pl.op("tensor", "matmul", pb[bk][:, 0:QT],
                                      wq[:, kc * 3 * D + which * D + mch * 128:
                                         kc * 3 * D + which * D + (mch + 1) * 128],
                                      lnta[:, kc * QT:(kc + 1) * QT],
                                      start=(kc == 0), stop=(kc == NCH - 1),
                                      waits=w, inc=("st" if kc == NCH - 1 else None))
                            tmm = pl.tok("st")
                            dslice = dstsb[:, mch * T + qb * QT:mch * T + (qb + 1) * QT]
                            if scale != 1.0:
                                tev = pl.op("vector", "tensor_scalar", dslice,
                                            pb[bk][:, 0:QT], scale, None,
                                            op0=ALU.mult, waits=[tmm], inc="sv")
                            else:
                                tev = pl.op("vector", "tensor_copy", dslice,
                                            pb[bk][:, 0:QT], waits=[tmm], inc="sv")
                            bank_free[bk] = tev
                            sq = sqhs[mch % 2]
                            tsq = pl.op("vector", "tensor_tensor", sq[:, :],
                                        dslice, dslice, ALU.mult,
                                        waits=[sqh_free[mch % 2]], inc="sv")
                            w = [tsq, esel_tok] + (
                                [bank_free[nbank]] if mch == 0 else [])
                            pl.op("tensor", "matmul", pb[nbank][0:128, 0:QT],
                                  Esel[:, mch * 128:(mch + 1) * 128], sq[:, :],
                                  start=(mch == 0), stop=(mch == NCH - 1),
                                  waits=w, inc="st")
                            sqh_free[mch % 2] = pl.tok("st")
                        tnm = pl.tok("st")
                        nsb = nq if which == 0 else nk
                        tnv = pl.op("vector", "tensor_copy",
                                    nsb[:, qb * QT:(qb + 1) * QT],
                                    pb[nbank][0:16, 0:QT], waits=[tnm], inc="sv")
                        bank_free[nbank] = tnv
                    for tc in range(4):
                        gtc = qb * 4 + tc
                        for vh in range(2):
                            bk = 2 + (tc * 2 + vh) % 2
                            for kc in range(NCH):
                                w = [lnta_ready[qb % 2], wqtok[kc]] + (
                                    [bank_free[bk]] if kc == 0 else [])
                                pl.op("tensor", "matmul", pb[bk][:, 0:QT],
                                      lnta[:, kc * QT + tc * 128:
                                           kc * QT + (tc + 1) * 128],
                                      wq[:, kc * 3 * D + 2 * D + vh * QT:
                                         kc * 3 * D + 2 * D + (vh + 1) * QT],
                                      start=(kc == 0), stop=(kc == NCH - 1),
                                      waits=w, inc=("st" if kc == NCH - 1 else None))
                            tmm = pl.tok("st")
                            tev = pl.op("vector", "tensor_copy",
                                        vsb4[:, gtc, vh * 8:(vh + 1) * 8, 0:64],
                                        pb[bk][:, 0:QT].rearrange(
                                            "p (h d) -> p h d", h=8),
                                        waits=[tmm], inc="sv")
                            bank_free[bk] = tev
                    lnta_free[qb % 2] = pl.tok("st")

                for qb in range(NQB + 1):
                    if qb < NQB:
                        emit_lnchain_a(qb)
                    if qb > 0:
                        emit_qkv(qb - 1)
                tkm = pl.op("vector", "reduce_max", kmax[:, :], nk[:, :],
                            axis=AX.X, inc="sv")
                pl.wait_only("vector", [tkm])
                tms_ = pl.op("vector", "tensor_scalar", nk[:, :], nq[:, :],
                             kmax[:, :], None, op0=ALU.mult, waits=[tkm], inc="sv")
                mall_tok = pl.op("scalar", "activation", mall[:, :], nk[:, :],
                                 AF.Sqrt, waits=[tms_], inc="sc")
                pl.flush()
            es1.close()

            es2 = contextlib.ExitStack()
            al2 = lambda n, sh, dt: es2.enter_context(nc.sbuf_tensor(n + "_s%d" % s, sh, dt))
            qst0 = al2("qst0", [128, T], BF16)
            woutsb = al2("woutsb", [128, NCH * D], BF16)
            kst0 = astat[:, KOFF:KOFF + T]
            pbuf0 = al2("pbuf0", [128, 12 * QT], BF16)
            pbuf1 = al2("pbuf1", [128, NTC * QT], BF16)
            opk = al2("opk", [128, NCH * T], BF16)
            zinv0 = al2("zinv0", [32, QT], BF16)
            zinv1 = al2("zinv1", [32, QT], BF16)
            x1c0 = al2("x1c0", [128, QT], F32)
            x2st0 = al2("x2st0", [128, QT], F32)

            if True:
                pl.barrier()
                pl.dma("sync", astat[:, ZCOFF:ZCOFF + 1024], cb_ZC.ap())
                zc_tok = pl.dma("sync", astat[0:32, Z2OFF:Z2OFF + 2048], cb_Z2C.ap())
                wotok = []
                for g in range(4):
                    wotok.append(pl.dma(
                        "gpsimd", woutsb[:, g * 2 * D:(g + 1) * 2 * D]
                        .rearrange("p (c m) -> p c m", c=2),
                        w_out.ap()[g * 256:(g + 1) * 256, :]
                        .rearrange("(c p) m -> p c m", p=128)))
                qst = [qst0]
                kst = [kst0]
                pl.op("gpsimd", "memset", qst0[64:128, :], 0.0, inc="sg")
                zpad = pl.op("gpsimd", "memset", kst0[64:128, :], -1.0, inc="sg")
                pbufs = [pbuf0, pbuf1]
                qst_free = [None]
                kst_free = [None]
                pbuf_free = [None, None]
                obank_free = [None, None]
                zfirst = [True, True]
                ptok = {}
                oev = {}
                qst_built = [None]
                kst_built = [None]
                rot = [0]

                def rot_next():
                    rot[0] = (rot[0] + 1) % 3
                    return rot[0]

                def build_staging(h):
                    mch = h // 2
                    psel = (h % 2) * 128
                    evq = evk = None
                    for blk in range(4):
                        for which in range(2):
                            src = qsb if which == 0 else ksb
                            dstt = qst[0] if which == 0 else kst[0]
                            fr = qst_free[0] if which == 0 else kst_free[0]
                            bk = rot_next()
                            pl.op("tensor", "matmul", pb[bk][0:128, 0:512],
                                  iselp[:, psel:psel + 128],
                                  src[:, mch * T + blk * 512:mch * T + (blk + 1) * 512],
                                  start=True, stop=True,
                                  waits=[consts_tok, fr, bank_free[bk]], inc="st")
                            ev = pl.op("vector", "tensor_copy",
                                       dstt[0:64, blk * 512:(blk + 1) * 512],
                                       pb[bk][0:64, 0:512],
                                       waits=[pl.tok("st")], inc="sv")
                            bank_free[bk] = ev
                            if which == 0:
                                evq = ev
                            else:
                                evk = ev
                    tdm = pl.dma("gpsimd", qst[0][64:65, :], mall[h:h + 1, :],
                                 waits=[mall_tok, qst_free[0], zpad])
                    qst_built[0] = (evq, tdm)
                    kst_built[0] = (evk, zpad)

                def gen_scores(h, qb):
                    pp = qb % 2
                    nkc = 4 * qb + 4
                    mz = [None]
                    for j in range(1, 4):
                        kc = 4 * qb + j
                        mz.append(pl.op("gpsimd", "memset",
                                        pbufs[pp][:, kc * QT:kc * QT + 128 * j],
                                        0.0, waits=[pbuf_free[pp]], inc="sg"))
                    for kc in range(nkc):
                        bk = rot_next()
                        w = list(qst_built[0]) + list(kst_built[0]) + [bank_free[bk]]
                        pl.op("tensor", "matmul", pb[bk][:, 0:QT],
                              kst[0][:, kc * 128:(kc + 1) * 128],
                              qst[0][:, qb * QT:(qb + 1) * QT],
                              start=True, stop=True, waits=w, inc="st")
                        tS = pl.tok("st")
                        j = kc - 4 * qb
                        if j >= 0:
                            texp = pl.op("scalar", "activation",
                                         pbufs[pp][:, kc * QT + 128 * j:(kc + 1) * QT],
                                         pb[bk][:, 128 * j:QT], AF.Exp,
                                         waits=[tS, pbuf_free[pp]], inc="sc")
                            ttri = pl.op("vector", "tensor_tensor",
                                         pbufs[pp][:, kc * QT + 128 * j:
                                                   kc * QT + 128 * (j + 1)],
                                         pbufs[pp][:, kc * QT + 128 * j:
                                                   kc * QT + 128 * (j + 1)],
                                         tri01[:, :], ALU.mult,
                                         waits=[texp, consts_tok], inc="sv")
                            ptok[(pp, kc)] = [texp, ttri, mz[j] if j >= 1 else None]
                        else:
                            texp = pl.op("scalar", "activation",
                                         pbufs[pp][:, kc * QT:(kc + 1) * QT],
                                         pb[bk][:, 0:QT], AF.Exp,
                                         waits=[tS, pbuf_free[pp]], inc="sc")
                            ptok[(pp, kc)] = [texp]
                        bank_free[bk] = texp
                        yield

                def gen_av(h, qb):
                    pp = qb % 2
                    nkc = 4 * qb + 4
                    odd = h % 2 == 1
                    ob = 3 + qb % 2
                    rows = slice(64, 128) if odd else slice(0, 64)
                    for kc in range(nkc):
                        w = list(ptok[(pp, kc)]) + (
                            [obank_free[qb % 2]] if kc == 0 else [])
                        vap = vsb[:, kc * 1024 + (h // 2) * 128:
                                  kc * 1024 + (h // 2) * 128 + 128]
                        pl.op("tensor", "matmul", pb[ob][0:128, 0:QT],
                              vap, pbufs[pp][:, kc * QT:(kc + 1) * QT],
                              start=(kc == 0), stop=(kc == nkc - 1),
                              waits=w, inc=("st" if kc == nkc - 1 else None))
                        yield
                    tav = pl.tok("st")
                    osl = opk[rows, (h // 2) * T + qb * QT:
                              (h // 2) * T + (qb + 1) * QT]
                    tOe = pl.op("vector", "tensor_copy", osl,
                                pb[ob][rows, 0:QT], waits=[tav], inc="sv")
                    obank_free[qb % 2] = tOe
                    oev[(h, qb)] = tOe

                def gen_z(h, qb):
                    pp = qb % 2
                    nkc = 4 * qb + 4
                    zbk = 5 + (qb % 2)
                    j = 16 * (qb >> 1) + h
                    for kc in range(nkc):
                        wz = [zc_tok] + list(ptok[(pp, kc)]) + (
                            [bank_free[zbk]] if zfirst[qb % 2] else [])
                        pl.op("tensor", "matmul", pb[zbk][0:32, 0:QT],
                              astat[:, ZCOFF + j * 32:ZCOFF + (j + 1) * 32],
                              pbufs[pp][:, kc * QT:(kc + 1) * QT],
                              start=zfirst[qb % 2],
                              stop=(h == H - 1 and qb >= 2 and kc == nkc - 1),
                              waits=wz, inc=("st" if kc == nkc - 1 else None))
                        zfirst[qb % 2] = False
                        yield
                    pbuf_free[pp] = pl.tok("st")

                def drive(gens, run=2):
                    gens = [g for g in gens if g is not None]
                    while gens:
                        nxt = []
                        for g in gens:
                            alive = True
                            for _ in range(run):
                                try:
                                    next(g)
                                except StopIteration:
                                    alive = False
                                    break
                            if alive:
                                nxt.append(g)
                        gens = nxt

                build_staging(0)
                prev = None
                for h in range(H):
                    for qb in range(NQB):
                        gs = gen_scores(h, qb)
                        if prev is not None:
                            drive([gs, gen_av(*prev), gen_z(*prev)])
                        else:
                            drive([gs])
                        if qb == NQB - 1:
                            qst_free[0] = pl.tok("st")
                            kst_free[0] = pl.tok("st")
                            if h + 1 < H:
                                build_staging(h + 1)
                        prev = (h, qb)
                drive([gen_av(*prev), gen_z(*prev)])

                t_zstop = pl.tok("st")
                zinvs = [zinv0, zinv1]
                trec = [None, None]
                for par in range(2):
                    trec[par] = pl.op("vector", "reciprocal", zinvs[par][:, :],
                                      pb[5 + par][0:32, 0:QT],
                                      waits=[t_zstop], inc="sv")
                    bank_free[5 + par] = trec[par]
                tnorm = {}
                for qb in range(NQB):
                    for kc in range(NCH):
                        bk = (qb * NCH + kc) % 2
                        slot = 8 * (qb >> 1) + kc
                        pl.op("tensor", "matmul", pb[bk][:, 0:QT],
                              astat[0:32, Z2OFF + slot * 128:Z2OFF + (slot + 1) * 128],
                              zinvs[qb % 2][:, :], start=True, stop=True,
                              waits=[trec[qb % 2], bank_free[bk], zc_tok],
                              inc="st")
                        osl = opk[:, kc * T + qb * QT:kc * T + (qb + 1) * QT]
                        tn = pl.op("vector", "tensor_tensor", osl, osl,
                                   pb[bk][:, 0:QT], ALU.mult,
                                   waits=[pl.tok("st"), oev[(2 * kc, qb)],
                                          oev[(2 * kc + 1, qb)]], inc="sv")
                        bank_free[bk] = tn
                        tnorm[(qb, kc)] = tn

                x1c = [x1c0]
                x2st = [x2st0]
                x1c_free = [None]
                x2st_free = [None]
                for qb in range(NQB):
                    for oc in range(NCH):
                        r = 0
                        tld = pl.dma("sync", x1c[r][:, :],
                                     x1s.ap()[s, oc, :, qb * QT:(qb + 1) * QT],
                                     waits=[x1c_free[r]])
                        bk = 7
                        for kc in range(NCH):
                            w = [wotok[kc // 2], tnorm[(qb, kc)]] + (
                                [bank_free[bk]] if kc == 0 else [])
                            pl.op("tensor", "matmul", pb[bk][:, 0:QT],
                                  woutsb[:, kc * D + oc * 128:kc * D + (oc + 1) * 128],
                                  opk[:, kc * T + qb * QT:kc * T + (qb + 1) * QT],
                                  start=(kc == 0), stop=(kc == NCH - 1),
                                  waits=w, inc=("st" if kc == NCH - 1 else None))
                        tmm = pl.tok("st")
                        tev = pl.op("vector", "scalar_tensor_tensor", x2st[r][:, :],
                                    pb[bk][:, 0:QT], bout_sb[:, oc:oc + 1],
                                    x1c[r][:, :], op0=ALU.add, op1=ALU.add,
                                    waits=[tmm, tld, x2st_free[r], consts_tok],
                                    inc="sv")
                        bank_free[bk] = tev
                        x1c_free[r] = tev
                        x2st_free[r] = pl.dma(
                            "gpsimd", x2s.ap()[s, oc, :, qb * QT:(qb + 1) * QT],
                            x2st[r][:, :], waits=[tev])
                pl.flush()
            es2.close()
        es0.close()
        for i in range(8):
            bank_free[i] = None

    ffn_phase(first=True)
    for s in range(S):
        attn_phase(s)
    ffn_phase(first=False)
    pl.barrier()
    pl.flush()
    stack.close()
    return nc


_NC_CACHE = {}


def _consts():
    ident = np.eye(128, dtype=np.float32)
    ones65 = np.zeros((65, 128), dtype=np.float32)
    ones65[0, :] = 1.0
    ones65[64, :] = 1.0
    isel = np.eye(128).astype(ml_dtypes.bfloat16)
    tri01 = np.triu(np.ones((128, 128))).astype(ml_dtypes.bfloat16)
    onescol = np.zeros((128, 128), dtype=ml_dtypes.bfloat16)
    onescol[:, 0] = 1.0
    E = np.zeros((128, 8 * 128), dtype=np.float32)
    for mch in range(8):
        for p in range(128):
            E[p, mch * 128 + 2 * mch + (p // 64)] = 1.0
    iselp = np.zeros((128, 256), dtype=np.float32)
    iselp[0:64, 0:64] = np.eye(64)
    iselp[64:128, 128:192] = np.eye(64)
    ZCm = np.zeros((128, 32 * 32), dtype=np.float32)
    for j in range(32):
        ZCm[:, j * 32 + j] = 1.0
    Z2Cm = np.zeros((32, 16 * 128), dtype=np.float32)
    for slot in range(16):
        for i in range(128):
            Z2Cm[2 * slot + i // 64, slot * 128 + i] = 1.0
    return (ident, ones65, isel, tri01, onescol, E.astype(ml_dtypes.bfloat16),
            ZCm.astype(ml_dtypes.bfloat16), Z2Cm.astype(ml_dtypes.bfloat16),
            iselp.astype(ml_dtypes.bfloat16))


def kernel(L0_attn_x, L0_attn_w_qkv, L0_attn_w_out, L0_attn_b_out,
           L0_ffn_W1, L0_ffn_b1, L0_ffn_W2, L0_ffn_b2):
    x = np.ascontiguousarray(np.asarray(L0_attn_x, dtype=np.float32))
    ident, ones65, isel, tri01, onescol, E, ZCm, Z2Cm, iselp_c = _consts()
    common = {
        "w_qkv": np.ascontiguousarray(np.asarray(L0_attn_w_qkv, dtype=np.float32)),
        "w_out": np.ascontiguousarray(np.asarray(L0_attn_w_out, dtype=np.float32)),
        "b_out": np.ascontiguousarray(np.asarray(L0_attn_b_out, dtype=np.float32)),
        "w1": np.ascontiguousarray(np.asarray(L0_ffn_W1, dtype=np.float32)),
        "b1": np.ascontiguousarray(np.asarray(L0_ffn_b1, dtype=np.float32)),
        "w2": np.ascontiguousarray(np.asarray(L0_ffn_W2, dtype=np.float32)),
        "b2": np.ascontiguousarray(np.asarray(L0_ffn_b2, dtype=np.float32)),
        "c_ident": ident, "c_ones65": ones65, "cb_isel": isel,
        "cb_tri01": tri01, "cb_onescol": onescol, "cb_E": E,
        "cb_ZC": ZCm, "cb_Z2C": Z2Cm, "cb_iselp": iselp_c,
    }
    if "nc" not in _NC_CACHE:
        _NC_CACHE["nc"] = build_program()
    nc = _NC_CACHE["nc"]
    in_maps = []
    for c in range(NCORES):
        m = dict(common)
        m["x"] = np.ascontiguousarray(x[c * S:(c + 1) * S])
        in_maps.append(m)
    res = run_bass_kernel_spmd(nc, in_maps, core_ids=list(range(NCORES)))
    return np.concatenate([np.asarray(r["out"]) for r in res.results], axis=0)
